# version 03
# baseline (speedup 1.0000x reference)
import math
import numpy as np
from contextlib import ExitStack

import concourse.bass as bass
import concourse.mybir as mybir
from concourse.bass_utils import run_bass_kernel_spmd

F32 = mybir.dt.float32
BF16 = mybir.dt.bfloat16
AF = mybir.ActivationFunctionType
ALU = mybir.AluOpType
AX = mybir.AxisListType

T = 2048
NT = 16
D = 1024
DC = 8
E = 8
F = 4096
NG = 4
GF = 8
TB = 256
NTB = 8
RANK = 64
N_ITER = 25
THRESH = 1e-9
EPS = 1e-5
SCALE = 1.0 / math.sqrt(D)
NGI = E * NG


class Ctr:
    def __init__(self, es, nc, name):
        self.h = es.enter_context(nc.semaphore(name))
        self.n = 0

    def inc(self, instr, k=1):
        instr.then_inc(self.h, k)
        self.n += k
        return self.n


def _finish_early(nc, block, OUT, acc, SV, SOUT, NT):
    @block.sync
    def _(sync):
        sync.wait_ge(SV.h, SV.n)
        SOUT.inc(sync.dma_start(out=OUT.rearrange("(i p) d -> p i d", p=128),
                                in_=acc.ap().rearrange("p (i d) -> p i d", i=NT)), 16)
        sync.wait_ge(SOUT.h, 16)
    return nc


def build(nc, dbg=False):
    PH = 99
    X = nc.dram_tensor("x", [T, D], F32, kind="ExternalInput").ap()
    Q = nc.dram_tensor("q", [RANK, D], F32, kind="ExternalInput").ap()
    KI = nc.dram_tensor("k", [RANK, D], F32, kind="ExternalInput").ap()
    VO = nc.dram_tensor("vo", [D, D], F32, kind="ExternalInput").ap()
    WR = nc.dram_tensor("wr", [D, E], F32, kind="ExternalInput").ap()
    BR = nc.dram_tensor("br", [1, E], F32, kind="ExternalInput").ap()
    W1 = nc.dram_tensor("w1", [E, D, F], F32, kind="ExternalInput").ap()
    B1 = nc.dram_tensor("b1", [E, F], F32, kind="ExternalInput").ap()
    W2 = nc.dram_tensor("w2", [E, F, D], F32, kind="ExternalInput").ap()
    B2 = nc.dram_tensor("b2", [E, D], F32, kind="ExternalInput").ap()
    ID32 = nc.dram_tensor("id32", [128, 128], F32, kind="ExternalInput").ap()
    OUT = nc.dram_tensor("out", [T, D], F32, kind="ExternalOutput").ap()
    if dbg:
        DBG_G = nc.dram_tensor("dbg_g", [128, NT * E], F32, kind="ExternalOutput").ap()
        DBG_A = nc.dram_tensor("dbg_a", [T, D], F32, kind="ExternalOutput").ap()
        DBG_XS = nc.dram_tensor("dbg_xs", [128, NT * E], F32, kind="ExternalOutput").ap()

    c = 0
    st_xnt = [0] * NT
    for i in range(NT):
        c += 1; st_xnt[i] = c
    c += 1; st_brbc = c
    st_logit = [0] * NT
    for i in range(NT):
        c += 1; st_logit[i] = c
    c += 1; st_qt = c
    c += 1; st_kt = c
    st_b1 = [0] * E
    for e in range(E):
        c += 1; st_b1[e] = c
    st_xq = [0] * 4
    for n in range(4):
        c += 1; st_xq[n] = c
    st_xk = [0] * 4
    for n in range(4):
        c += 1; st_xk[n] = c
    st_v = [0] * 32
    for k in range(32):
        c += 1; st_v[k] = c
    st_sc = [0] * NT; st_pt = [0] * NT; st_pv = [0] * NT
    for i in range(NT):
        c += 1; st_sc[i] = c
        c += 1; st_pt[i] = c
        c += 1; st_pv[i] = c
    st_gT = [0] * NT
    for i in range(NT):
        c += 1; st_gT[i] = c
    st_gb2 = [0] * 32
    for k in range(32):
        c += 1; st_gb2[k] = c
    st_h = {}; st_B = {}; st_wfree = [0] * NGI
    for gi in range(NGI):
        for tb in range(NTB):
            for fc in range(GF):
                c += 1; st_h[(gi, tb, fc)] = c
            if tb >= 1:
                c += 1; st_B[(gi, tb - 1)] = c
        c += 1; st_B[(gi, NTB - 1)] = c
        st_wfree[gi] = c
    st_total = c

    c = 0
    c += 1; s_rstd = c
    ss_exp = [0] * NT
    for i in range(NT):
        c += 1; ss_exp[i] = c
    ss_hev = {}
    for gi in range(NGI):
        for tb in range(NTB):
            for fc in range(GF):
                c += 1; ss_hev[(gi, tb, fc)] = c
    ss_total = c

    c = 0
    c += 2; v_pre = c
    c += 1; v_negmean = c
    sv_xn_ts = [0] * NT; sv_xn_ev = [0] * NT
    xn_order = [("ts", 0), ("ts", 1)]
    for i in range(2, NT + 2):
        xn_order.append(("ev", i - 2))
        if i < NT:
            xn_order.append(("ts", i))
    for kind, idx in xn_order:
        c += 1
        if kind == "ts":
            sv_xn_ts[idx] = c
        else:
            sv_xn_ev[idx] = c
    c += 1; sv_brb = c
    sv_xs = [0] * NT
    for i in range(NT):
        c += 1; sv_xs[i] = c
    c += 1; sv_qk = c
    sv_b1ev = [0] * E
    for e in range(E):
        c += 1; sv_b1ev[e] = c
    sv_xqe = [0] * 4
    for n in range(4):
        c += 1; sv_xqe[n] = c
    sv_xke = [0] * 4
    for n in range(4):
        c += 1; sv_xke[n] = c
    sv_ve = [0] * 32
    for k in range(32):
        c += 1; sv_ve[k] = c
    sv_aux_done = c
    c += 4 + N_ITER * 5 + 3
    sv_gates = c
    sv_m = [0] * NT; sv_pte = [0] * NT; sv_oev = [0] * NT
    for i in range(NT):
        c += 3; sv_m[i] = c
        c += 1
        c += 1; sv_pte[i] = c
        c += 1; sv_oev[i] = c
    sv_gTev = [0] * NT
    for i in range(NT):
        c += 1; sv_gTev[i] = c
    sv_gb2 = [0] * 32
    for k in range(32):
        c += 1; sv_gb2[k] = c
    sv_all_A = c
    sv_yev = {}
    for gi in range(NGI):
        for tb in range(NTB):
            c += 1; sv_yev[(gi, tb)] = c
    sv_final = c

    es = ExitStack()
    sb = lambda n, s, d: es.enter_context(nc.sbuf_tensor(n, s, d))
    acc = sb("acc", [128, NT * D], F32)
    xnT = sb("xnT", [128, DC * T], BF16)
    gates = sb("gates", [128, NT * E], F32)
    b1sb = sb("b1sb", [128, E * 32], F32)
    b1raw = sb("b1raw", [32, E * 128], F32)
    ident_bf = sb("ident_bf", [128, 128], BF16)
    ident_f32 = sb("ident_f32", [128, 128], F32)
    wrbf = sb("wrbf", [128, DC * E], BF16)
    brpad = sb("brpad", [128, E], BF16)
    brb_half = sb("brb_half", [128, E], F32)
    e0row = sb("e0row", [128, 128], BF16)
    b2p = sb("b2p", [128, D], BF16)
    gTp = sb("gTp", [128, T], BF16)
    sums = sb("sums", [128, NT], F32)
    negmean = sb("negmean", [128, NT], F32)
    sumsq = sb("sumsq", [128, NT], F32)
    lnv = sb("lnv", [128, NT], F32)
    rstd = sb("rstd", [128, NT], F32)
    junk = sb("junk", [128, D], BF16)
    Xs = sb("Xs", [128, NT * E], F32)
    mx = sb("mx", [128, NT], F32)
    tauL = sb("tauL", [128, NT], F32)
    tauM = sb("tauM", [128, NT], F32)
    sL = sb("sL", [128, NT], F32)
    fLo = sb("fLo", [128, NT], F32)
    sM = sb("sM", [128, NT], F32)
    prod = sb("prod", [128, NT], F32)
    delta = sb("delta", [128, NT], F32)
    rbuf = sb("rbuf", [128, NT * E], F32)
    sqbuf = sb("sqbuf", [128, NT * E], F32)
    pbuf = sb("pbuf", [128, NT * E], F32)
    t1buf = sb("t1buf", [128, NT * E], F32)
    maskbuf = sb("maskbuf", [128, NT * E], F32)
    Zg = sb("Zg", [128, NT], F32)
    Zinv = sb("Zinv", [128, NT], F32)
    mxc4 = sb("mxc4", [128, NT * 4], F32)
    mxr = sb("mxr", [128, NT], F32)
    negmax = sb("negmax", [128, NT], F32)
    zp = sb("zp", [128, NT * 4], F32)
    Zcol = sb("Zcol", [128, NT], F32)
    Zvinv = sb("Zvinv", [128, NT], F32)
    zerocol = sb("zerocol", [128, 1], F32)
    epscol = sb("epscol", [128, 1], F32)

    sems = ExitStack()
    SY = Ctr(sems, nc, "SY")
    GP = Ctr(sems, nc, "GP")
    WD = [Ctr(sems, nc, "WD0"), Ctr(sems, nc, "WD1")]
    SOUT = Ctr(sems, nc, "SOUT")
    SV = Ctr(sems, nc, "SV")
    SS = Ctr(sems, nc, "SS")
    ST = Ctr(sems, nc, "ST")
    SD = Ctr(sems, nc, "SD")
    SB = Ctr(sems, nc, "SB")
    SB2 = Ctr(sems, nc, "SB2")

    block = es.enter_context(nc.Block())

    esA = ExitStack()
    sbA = lambda n, s, d: esA.enter_context(nc.sbuf_tensor(n, s, d))
    xnb = [sbA("xnb0", [128, D], BF16), sbA("xnb1", [128, D], BF16)]
    qbf = sbA("qbf", [RANK, D], BF16)
    kbf = sbA("kbf", [RANK, D], BF16)
    QTp = sbA("QTp", [128, DC * 128], BF16)
    KTp = sbA("KTp", [128, DC * 128], BF16)
    xqT = sbA("xqT", [128, T], BF16)
    xkT = sbA("xkT", [128, T], BF16)
    vobf = sbA("vobf", [128, DC * D], BF16)
    v_sb = sbA("v_sb", [128, NT * D], BF16)
    P_sb = sbA("P_sb", [128, T], BF16)
    PT_sb = sbA("PT_sb", [128, T], BF16)

    esP1 = ExitStack()
    psA = lambda n, s, d: esP1.enter_context(nc.psum_tensor(n, s, d))
    pt_xn = psA("pt_xn", [128, 1024], BF16)
    pmisc = psA("pmisc", [128, 1024], BF16)
    pmisc32 = psA("pmisc32", [128, 512], F32)
    ps_big = psA("ps_big", [128, 2048], F32)

    @block.sync
    def _(sync):
        SY.inc(sync.dma_start(out=acc.ap().rearrange("p (i d) -> p i d", i=NT),
                              in_=X.rearrange("(i p) d -> p i d", p=128)), 16)
        SY.inc(sync.dma_start(out=b1raw.ap().rearrange("j (e c) -> j e c", e=E),
                              in_=B1.rearrange("e (j c) -> j e c", c=128)), 16)
        SY.inc(sync.dma_start(out=ident_f32[:, :], in_=ID32[:, :]), 16)
    sy_setup = SY.n

    @block.vector
    def _(vector):
        nc.vector.memset(zerocol[:, :], 0.0)
        nc.vector.memset(epscol[:, :], EPS)
        SV.inc(nc.vector.memset(e0row[:, :], 0.0))
        vector.wait_ge(SV.h, SV.n)
        nc.vector.memset(e0row[0:1, :], 1.0)
        nc.vector.memset(brpad[:, :], 0.0)
        nc.vector.memset(b2p[:, :], 0.0)
        nc.vector.memset(gTp[:, :], 0.0)
        nc.vector.memset(QTp[:, :], 0.0)
        assert SV.inc(nc.vector.memset(KTp[:, :], 0.0)) == v_pre

    @block.gpsimd
    def _(gpsimd):
        GP.inc(gpsimd.dma_start(out=ident_bf[:, :], in_=ID32[:, :]), 16)
        GP.inc(gpsimd.dma_start(out=qbf[:, :], in_=Q[:, :]), 16)
        GP.inc(gpsimd.dma_start(out=kbf[:, :], in_=KI[:, :]), 16)
        GP.inc(gpsimd.dma_start(out=vobf.ap().rearrange("p (dc e) -> p dc e", dc=DC),
                                in_=VO.rearrange("(dc p) e -> p dc e", p=128)), 16)
        GP.inc(gpsimd.dma_start(out=wrbf.ap().rearrange("p (dc e) -> p dc e", dc=DC),
                                in_=WR.rearrange("(dc p) e -> p dc e", p=128)), 16)
        gpsimd.wait_ge(SV.h, v_pre)
        GP.inc(gpsimd.dma_start(out=brpad[0:1, :], in_=BR[:, :]), 16)
        GP.inc(gpsimd.dma_start(out=b2p[0:8, :], in_=B2[:, :]), 16)
    gp_setup = GP.n

    @block.vector
    def _(vector):
        vector.wait_ge(SY.h, sy_setup)
        for i in range(NT):
            ins = nc.vector.reduce_sum(sums[:, i:i + 1], acc[:, i * D:(i + 1) * D], axis=AX.X)
        SB.inc(ins); vector.wait_ge(SB.h, SB.n)
        for i in range(NT):
            ins = nc.vector.tensor_scalar(negmean[:, i:i + 1], sums[:, i:i + 1],
                                          -1.0 / D, None, op0=ALU.mult)
        assert SV.inc(ins) == v_negmean

    @block.scalar
    def _(scalar):
        scalar.wait_ge(SV.h, v_negmean)
        for i in range(NT):
            ins = nc.scalar.activation(junk[:, :], acc[:, i * D:(i + 1) * D], AF.Square,
                                       bias=negmean[:, i:i + 1], scale=1.0,
                                       accum_out=sumsq[:, i:i + 1])
            SB2.inc(ins); scalar.wait_ge(SB2.h, SB2.n)
        for i in range(NT):
            ins = nc.scalar.activation(lnv[:, i:i + 1], sumsq[:, i:i + 1], AF.Ln,
                                       bias=epscol[:, :], scale=1.0 / D)
        SB2.inc(ins); scalar.wait_ge(SB2.h, SB2.n)
        for i in range(NT):
            ins = nc.scalar.activation(rstd[:, i:i + 1], lnv[:, i:i + 1], AF.Exp,
                                       bias=zerocol[:, :], scale=-0.5)
        assert SS.inc(ins) == s_rstd

    @block.vector
    def _(vector):
        vector.wait_ge(SS.h, s_rstd)
        for kind, i in xn_order:
            if kind == "ts":
                if i >= 2:
                    vector.wait_ge(ST.h, st_xnt[i - 2])
                ins = nc.vector.tensor_scalar(xnb[i % 2][:, :], acc[:, i * D:(i + 1) * D],
                                              negmean[:, i:i + 1], rstd[:, i:i + 1],
                                              op0=ALU.add, op1=ALU.mult)
                assert SV.inc(ins) == sv_xn_ts[i]
            else:
                vector.wait_ge(ST.h, st_xnt[i])
                for j in range(DC):
                    ins = nc.vector.tensor_copy(
                        xnT[:, j * T + i * 128: j * T + i * 128 + 128],
                        pt_xn[:, j * 128:(j + 1) * 128])
                assert SV.inc(ins) == sv_xn_ev[i]

    @block.tensor
    def _(tensor):
        tensor.wait_ge(GP.h, gp_setup)
        for i in range(NT):
            tensor.wait_ge(SV.h, sv_xn_ts[i])
            if i >= 1:
                tensor.wait_ge(SV.h, sv_xn_ev[i - 1])
            for j in range(DC):
                ins = nc.tensor.transpose(pt_xn[:, j * 128:(j + 1) * 128],
                                          xnb[i % 2][:, j * 128:(j + 1) * 128],
                                          ident_bf[:, :])
            assert ST.inc(ins) == st_xnt[i]
        ins = nc.tensor.matmul(pmisc32[:, 448:448 + E], e0row[:, :], brpad[:, :],
                               start=True, stop=True)
        assert ST.inc(ins) == st_brbc
        for i in range(NT):
            tensor.wait_ge(SV.h, sv_xn_ev[i])
            if i >= 4:
                tensor.wait_ge(SV.h, sv_xs[i - 4])
            base = (i % 4) * 512
            for dc in range(DC):
                ins = nc.tensor.matmul(ps_big[:, base:base + E],
                                       xnT[:, dc * T + i * 128: dc * T + i * 128 + 128],
                                       wrbf[:, dc * E:(dc + 1) * E],
                                       start=(dc == 0), stop=(dc == DC - 1))
            assert ST.inc(ins) == st_logit[i]
        for dc in range(DC):
            ins = nc.tensor.transpose(pmisc[:, dc * 64:(dc + 1) * 64],
                                      qbf[:, dc * 128:(dc + 1) * 128],
                                      ident_bf[0:64, 0:64])
        assert ST.inc(ins) == st_qt
        for dc in range(DC):
            ins = nc.tensor.transpose(pmisc[:, 512 + dc * 64: 512 + (dc + 1) * 64],
                                      kbf[:, dc * 128:(dc + 1) * 128],
                                      ident_bf[0:64, 0:64])
        assert ST.inc(ins) == st_kt
        tensor.wait_ge(SY.h, sy_setup)
        tensor.wait_ge(SV.h, sv_brb)
        for e in range(E):
            if e >= 1:
                tensor.wait_ge(SV.h, sv_b1ev[e - 1])
            ins = nc.tensor.transpose(pmisc32[:, (e % 4) * 128:(e % 4) * 128 + 32],
                                      b1raw[:, e * 128:(e + 1) * 128],
                                      ident_f32[0:32, 0:32])
            assert ST.inc(ins) == st_b1[e]

    @block.vector
    def _(vector):
        vector.wait_ge(ST.h, st_brbc)
        ins = nc.vector.tensor_scalar(brb_half[:, :], pmisc32[:, 448:448 + E],
                                      0.5, None, op0=ALU.mult)
        assert SV.inc(ins) == sv_brb
        vector.wait_ge(SV.h, sv_brb)
        for i in range(NT):
            vector.wait_ge(ST.h, st_logit[i])
            base = (i % 4) * 512
            ins = nc.vector.scalar_tensor_tensor(Xs[:, i * E:(i + 1) * E],
                                                 ps_big[:, base:base + E],
                                                 0.5, brb_half[:, :],
                                                 op0=ALU.mult, op1=ALU.add)
            assert SV.inc(ins) == sv_xs[i]
        vector.wait_ge(ST.h, st_kt)
        vector.wait_ge(SV.h, v_pre)
        for dc in range(DC):
            nc.vector.tensor_copy(QTp[:, dc * 128: dc * 128 + 64],
                                  pmisc[:, dc * 64:(dc + 1) * 64])
        for dc in range(DC):
            ins = nc.vector.tensor_copy(KTp[:, dc * 128: dc * 128 + 64],
                                        pmisc[:, 512 + dc * 64: 512 + (dc + 1) * 64])
        assert SV.inc(ins) == sv_qk
        for e in range(E):
            vector.wait_ge(ST.h, st_b1[e])
            ins = nc.vector.tensor_copy(b1sb[:, e * 32:(e + 1) * 32],
                                        pmisc32[:, (e % 4) * 128:(e % 4) * 128 + 32])
            assert SV.inc(ins) == sv_b1ev[e]

    if PH < 4:
        return _finish_early(nc, block, OUT, acc, SV, SOUT, NT)
    @block.tensor
    def _(tensor):
        tensor.wait_ge(SV.h, sv_xs[NT - 1])
        tensor.wait_ge(SV.h, sv_qk)
        for n in range(4):
            for dc in range(DC):
                ins = nc.tensor.matmul(ps_big[:, n * 512:(n + 1) * 512],
                                       QTp[:, dc * 128:(dc + 1) * 128],
                                       xnT[:, dc * T + n * 512: dc * T + (n + 1) * 512],
                                       start=(dc == 0), stop=(dc == DC - 1))
            assert ST.inc(ins) == st_xq[n]
        for n in range(4):
            tensor.wait_ge(SV.h, sv_xqe[n])
            for dc in range(DC):
                ins = nc.tensor.matmul(ps_big[:, n * 512:(n + 1) * 512],
                                       KTp[:, dc * 128:(dc + 1) * 128],
                                       xnT[:, dc * T + n * 512: dc * T + (n + 1) * 512],
                                       start=(dc == 0), stop=(dc == DC - 1))
            assert ST.inc(ins) == st_xk[n]
        for k in range(32):
            ut, eh = divmod(k, 2)
            slot = k % 4
            tensor.wait_ge(SV.h, sv_xke[slot] if k < 4 else sv_ve[k - 4])
            for dc in range(DC):
                ins = nc.tensor.matmul(
                    ps_big[:, slot * 512:(slot + 1) * 512],
                    xnT[:, dc * T + ut * 128: dc * T + ut * 128 + 128],
                    vobf[:, dc * D + eh * 512: dc * D + eh * 512 + 512],
                    start=(dc == 0), stop=(dc == DC - 1))
            assert ST.inc(ins) == st_v[k]

    @block.vector
    def _(vector):
        for n in range(4):
            vector.wait_ge(ST.h, st_xq[n])
            ins = nc.vector.tensor_scalar(xqT[:, n * 512:(n + 1) * 512],
                                          ps_big[:, n * 512:(n + 1) * 512],
                                          SCALE, None, op0=ALU.mult)
            assert SV.inc(ins) == sv_xqe[n]
        for n in range(4):
            vector.wait_ge(ST.h, st_xk[n])
            ins = nc.vector.tensor_copy(xkT[:, n * 512:(n + 1) * 512],
                                        ps_big[:, n * 512:(n + 1) * 512])
            assert SV.inc(ins) == sv_xke[n]
        for k in range(32):
            ut, eh = divmod(k, 2)
            slot = k % 4
            vector.wait_ge(ST.h, st_v[k])
            ins = nc.vector.tensor_copy(
                v_sb[:, ut * D + eh * 512: ut * D + eh * 512 + 512],
                ps_big[:, slot * 512:(slot + 1) * 512])
            assert SV.inc(ins) == sv_ve[k]
        assert SV.n == sv_aux_done

    if PH < 5:
        return _finish_early(nc, block, OUT, acc, SV, SOUT, NT)
    def vbar(vector, ins):
        SV.inc(ins)
        vector.wait_ge(SV.h, SV.n)

    @block.vector
    def _(vector):
        vector.wait_ge(SV.h, sv_xs[NT - 1])
        for i in range(NT):
            ins = nc.vector.reduce_max(mx[:, i:i + 1], Xs[:, i * E:(i + 1) * E], axis=AX.X)
        vbar(vector, ins)
        ins = nc.vector.tensor_scalar(tauL[:, :], mx[:, :], -1.0, None, op0=ALU.add)
        vbar(vector, ins)
        for i in range(NT):
            ins = nc.vector.tensor_scalar(rbuf[:, i * E:(i + 1) * E], Xs[:, i * E:(i + 1) * E],
                                    tauL[:, i:i + 1], 0.0, op0=ALU.subtract, op1=ALU.max)
        SB.inc(ins); vector.wait_ge(SB.h, SB.n)
        for i in range(NT):
            ins = nc.vector.tensor_tensor(sqbuf[:, i * E:(i + 1) * E], rbuf[:, i * E:(i + 1) * E],
                                    rbuf[:, i * E:(i + 1) * E], op=ALU.mult)
        SB.inc(ins); vector.wait_ge(SB.h, SB.n)
        for i in range(NT):
            ins = nc.vector.reduce_sum(sL[:, i:i + 1], sqbuf[:, i * E:(i + 1) * E], axis=AX.X)
        vbar(vector, ins)
        ins = nc.vector.tensor_scalar(fLo[:, :], sL[:, :], -1.0, None, op0=ALU.add)
        vbar(vector, ins)
        dm = 1.0 - (1.0 / E) ** 0.5
        for it in range(N_ITER):
            dm = dm * 0.5
            ins = nc.vector.tensor_scalar(tauM[:, :], tauL[:, :], dm, None, op0=ALU.add)
            vbar(vector, ins)
            for i in range(NT):
                ins = nc.vector.tensor_scalar(rbuf[:, i * E:(i + 1) * E],
                                        Xs[:, i * E:(i + 1) * E],
                                        tauM[:, i:i + 1], 0.0,
                                        op0=ALU.subtract, op1=ALU.max)
            SB.inc(ins); vector.wait_ge(SB.h, SB.n)
            for i in range(NT):
                ins = nc.vector.tensor_tensor(sqbuf[:, i * E:(i + 1) * E],
                                        rbuf[:, i * E:(i + 1) * E],
                                        rbuf[:, i * E:(i + 1) * E], op=ALU.mult)
            SB.inc(ins); vector.wait_ge(SB.h, SB.n)
            for i in range(NT):
                ins = nc.vector.reduce_sum(sM[:, i:i + 1],
                                           sqbuf[:, i * E:(i + 1) * E], axis=AX.X)
            vbar(vector, ins)
            ins = nc.vector.scalar_tensor_tensor(prod[:, :], sM[:, :], -1.0, fLo[:, :],
                                                 op0=ALU.add, op1=ALU.mult)
            vbar(vector, ins)
            ins = nc.vector.tensor_scalar(delta[:, :], prod[:, :], 0.0, dm,
                                          op0=ALU.is_ge, op1=ALU.mult)
            vbar(vector, ins)
            ins = nc.vector.tensor_tensor(tauL[:, :], tauL[:, :], delta[:, :], op=ALU.add)
            vbar(vector, ins)
        for i in range(NT):
            ins = nc.vector.tensor_scalar(rbuf[:, i * E:(i + 1) * E], Xs[:, i * E:(i + 1) * E],
                                    tauM[:, i:i + 1], 0.0, op0=ALU.subtract, op1=ALU.max)
        SB.inc(ins); vector.wait_ge(SB.h, SB.n)
        for i in range(NT):
            ins = nc.vector.tensor_tensor(pbuf[:, i * E:(i + 1) * E], rbuf[:, i * E:(i + 1) * E],
                                    rbuf[:, i * E:(i + 1) * E], op=ALU.mult)
        SB.inc(ins); vector.wait_ge(SB.h, SB.n)
        for i in range(NT):
            ins = nc.vector.reduce_sum(Zg[:, i:i + 1], pbuf[:, i * E:(i + 1) * E], axis=AX.X)
        vbar(vector, ins)
        ins = nc.vector.reciprocal(Zinv[:, :], Zg[:, :])
        vbar(vector, ins)
        for i in range(NT):
            ins = nc.vector.tensor_scalar(t1buf[:, i * E:(i + 1) * E], pbuf[:, i * E:(i + 1) * E],
                                    Zinv[:, i:i + 1], None, op0=ALU.mult)
        SB.inc(ins); vector.wait_ge(SB.h, SB.n)
        for i in range(NT):
            ins = nc.vector.tensor_scalar(maskbuf[:, i * E:(i + 1) * E],
                                    t1buf[:, i * E:(i + 1) * E],
                                    THRESH, None, op0=ALU.is_gt)
        SB.inc(ins); vector.wait_ge(SB.h, SB.n)
        for i in range(NT):
            ins = nc.vector.tensor_tensor(gates[:, i * E:(i + 1) * E],
                                          t1buf[:, i * E:(i + 1) * E],
                                          maskbuf[:, i * E:(i + 1) * E], op=ALU.mult)
        assert SV.inc(ins) == sv_gates

    if PH < 6:
        return _finish_early(nc, block, OUT, acc, SV, SOUT, NT)
    esP1.close()
    esP2 = ExitStack()
    psB = lambda n, s, d: esP2.enter_context(nc.psum_tensor(n, s, d))
    ps_s = psB("ps_s", [128, 2048], F32)
    ps_pt = psB("ps_pt", [128, 2048], BF16)
    ps_o = psB("ps_o", [128, 1024], F32)

    @block.tensor
    def _(tensor):
        tensor.wait_ge(SV.h, sv_aux_done)
        for i in range(NT):
            if i >= 1:
                tensor.wait_ge(SS.h, ss_exp[i - 1])
            for n in range(4):
                ins = nc.tensor.matmul(ps_s[:, n * 512:(n + 1) * 512],
                                       xqT[:, i * 128:(i + 1) * 128],
                                       xkT[:, n * 512:(n + 1) * 512],
                                       start=True, stop=True)
            assert ST.inc(ins) == st_sc[i]
            tensor.wait_ge(SS.h, ss_exp[i])
            if i >= 1:
                tensor.wait_ge(SV.h, sv_pte[i - 1])
            for j in range(NT):
                ins = nc.tensor.transpose(ps_pt[:, j * 128:(j + 1) * 128],
                                          P_sb[:, j * 128:(j + 1) * 128],
                                          ident_bf[:, :])
            assert ST.inc(ins) == st_pt[i]
            tensor.wait_ge(SV.h, sv_pte[i])
            if i >= 1:
                tensor.wait_ge(SV.h, sv_oev[i - 1])
            for eh in range(2):
                for uc in range(NT):
                    ins = nc.tensor.matmul(
                        ps_o[:, eh * 512:(eh + 1) * 512],
                        PT_sb[:, uc * 128:(uc + 1) * 128],
                        v_sb[:, uc * D + eh * 512: uc * D + eh * 512 + 512],
                        start=(uc == 0), stop=(uc == NT - 1))
            assert ST.inc(ins) == st_pv[i]

    @block.scalar
    def _(scalar):
        for i in range(NT):
            scalar.wait_ge(SV.h, sv_m[i])
            if i >= 1:
                scalar.wait_ge(ST.h, st_pt[i - 1])
            for n in range(4):
                ins = nc.scalar.activation(P_sb[:, n * 512:(n + 1) * 512],
                                           ps_s[:, n * 512:(n + 1) * 512], AF.Exp,
                                           bias=negmax[:, i:i + 1], scale=1.0,
                                           accum_out=zp[:, i * 4 + n: i * 4 + n + 1])
            assert SS.inc(ins) == ss_exp[i]

    @block.vector
    def _(vector):
        for i in range(NT):
            vector.wait_ge(ST.h, st_sc[i])
            for n in range(4):
                ins = nc.vector.reduce_max(mxc4[:, i * 4 + n: i * 4 + n + 1],
                                           ps_s[:, n * 512:(n + 1) * 512], axis=AX.X)
            vbar(vector, ins)
            ins = nc.vector.reduce_max(mxr[:, i:i + 1], mxc4[:, i * 4:(i + 1) * 4],
                                       axis=AX.X)
            vbar(vector, ins)
            ins = nc.vector.tensor_scalar(negmax[:, i:i + 1], mxr[:, i:i + 1],
                                          -1.0, None, op0=ALU.mult)
            assert SV.inc(ins) == sv_m[i]
            vector.wait_ge(SS.h, ss_exp[i])
            ins = nc.vector.reduce_sum(Zcol[:, i:i + 1], zp[:, i * 4:(i + 1) * 4],
                                       axis=AX.X)
            vbar(vector, ins)
            ins = nc.vector.reciprocal(Zvinv[:, i:i + 1], Zcol[:, i:i + 1])
            SB.inc(ins); vector.wait_ge(SB.h, SB.n)
            vector.wait_ge(ST.h, st_pt[i])
            ins = nc.vector.tensor_copy(PT_sb[:, :], ps_pt[:, :])
            assert SV.inc(ins) == sv_pte[i]
            vector.wait_ge(ST.h, st_pv[i])
            for eh in range(2):
                ins = nc.vector.scalar_tensor_tensor(
                    acc[:, i * D + eh * 512: i * D + eh * 512 + 512],
                    ps_o[:, eh * 512:(eh + 1) * 512],
                    Zvinv[:, i:i + 1],
                    acc[:, i * D + eh * 512: i * D + eh * 512 + 512],
                    op0=ALU.mult, op1=ALU.add)
            assert SV.inc(ins) == sv_oev[i]

    if PH < 7:
        return _finish_early(nc, block, OUT, acc, SV, SOUT, NT)
    @block.tensor
    def _(tensor):
        tensor.wait_ge(SV.h, sv_gates)
        tensor.wait_ge(SS.h, ss_exp[NT - 1])
        for i in range(NT):
            if i >= 4:
                tensor.wait_ge(SV.h, sv_gTev[i - 4])
            ins = nc.tensor.transpose(ps_s[0:8, (i % 4) * 512:(i % 4) * 512 + 128],
                                      gates[:, i * E:(i + 1) * E], ident_f32[:, :])
            assert ST.inc(ins) == st_gT[i]
        for k in range(32):
            ti, dh = divmod(k, 2)
            tensor.wait_ge(SV.h, sv_gTev[ti])
            if k < 2:
                tensor.wait_ge(SV.h, sv_oev[NT - 1])
            else:
                tensor.wait_ge(SV.h, sv_gb2[k - 2])
            ins = nc.tensor.matmul(ps_o[:, dh * 512:(dh + 1) * 512],
                                   gTp[:, ti * 128:(ti + 1) * 128],
                                   b2p[:, dh * 512:(dh + 1) * 512],
                                   start=True, stop=True)
            assert ST.inc(ins) == st_gb2[k]

    @block.vector
    def _(vector):
        vector.wait_ge(SV.h, v_pre)
        for i in range(NT):
            vector.wait_ge(ST.h, st_gT[i])
            ins = nc.vector.tensor_copy(gTp[0:8, i * 128:(i + 1) * 128],
                                        ps_s[0:8, (i % 4) * 512:(i % 4) * 512 + 128])
            assert SV.inc(ins) == sv_gTev[i]
        for k in range(32):
            ti, dh = divmod(k, 2)
            vector.wait_ge(ST.h, st_gb2[k])
            ins = nc.vector.tensor_tensor(
                acc[:, ti * D + dh * 512: ti * D + dh * 512 + 512],
                acc[:, ti * D + dh * 512: ti * D + dh * 512 + 512],
                ps_o[:, dh * 512:(dh + 1) * 512], op=ALU.add)
            assert SV.inc(ins) == sv_gb2[k]
        assert SV.n == sv_all_A

    if dbg:
        @block.sync
        def _(sync):
            sync.wait_ge(SV.h, sv_all_A)
            SD.inc(sync.dma_start(out=DBG_G[:, :], in_=gates[:, :]), 16)
            SD.inc(sync.dma_start(out=DBG_XS[:, :], in_=Xs[:, :]), 16)
            SD.inc(sync.dma_start(out=DBG_A.rearrange("(i p) d -> p i d", p=128),
                                  in_=acc.ap().rearrange("p (i d) -> p i d", i=NT)), 16)
    sd_dbg = SD.n

    if PH < 8:
        return _finish_early(nc, block, OUT, acc, SV, SOUT, NT)
    esP2.close()
    esA.close()
    esB = ExitStack()
    sbB = lambda n, s, d: esB.enter_context(nc.sbuf_tensor(n, s, d))
    psC = lambda n, s, d: esB.enter_context(nc.psum_tensor(n, s, d))
    w1bf = [sbB("w1bf0", [128, GF * D], BF16), sbB("w1bf1", [128, GF * D], BF16)]
    w2bf = [sbB("w2bf0", [128, GF * D], BF16), sbB("w2bf1", [128, GF * D], BF16)]
    h_grp = [sbB("hg0", [128, GF * TB], BF16), sbB("hg1", [128, GF * TB], BF16)]
    ps_h = [psC("ps_h0", [128, 512], F32), psC("ps_h1", [128, 512], F32)]
    ps_y = psC("ps_y", [128, 2048], F32)

    W1v = W1.rearrange("e (dc p) f -> e p dc f", p=128)
    W2v = W2.rearrange("e (fc p) d -> e p fc d", p=128)

    @block.gpsimd
    def _(gpsimd):
        for gi in range(NGI):
            e, g = divmod(gi, NG)
            par = gi % 2
            if gi < 2:
                gpsimd.wait_ge(SV.h, sv_all_A)
                if dbg:
                    gpsimd.wait_ge(SD.h, sd_dbg)
            else:
                gpsimd.wait_ge(ST.h, st_wfree[gi - 2])
            WD[par].inc(gpsimd.dma_start(
                out=w1bf[par].ap().rearrange("p (dc f) -> p dc f", dc=DC),
                in_=W1v[e, :, :, g * 1024:(g + 1) * 1024]), 16)
            WD[par].inc(gpsimd.dma_start(
                out=w2bf[par].ap().rearrange("p (fc d) -> p fc d", fc=GF),
                in_=W2v[e, :, g * GF:(g + 1) * GF, :]), 16)

    @block.tensor
    def _(tensor):
        tensor.wait_ge(SV.h, sv_all_A)

        def emit_A(gi, tb):
            par = gi % 2
            for fc in range(GF):
                lin = (gi * NTB + tb) * GF + fc - 2
                if lin >= 0:
                    pgi, rem = divmod(lin, NTB * GF)
                    ptb, pfc = divmod(rem, GF)
                    tensor.wait_ge(SS.h, ss_hev[(pgi, ptb, pfc)])
                for dc in range(DC):
                    ins = nc.tensor.matmul(
                        ps_h[fc % 2][:, 0:TB],
                        w1bf[par][:, dc * 1024 + fc * 128: dc * 1024 + fc * 128 + 128],
                        xnT[:, dc * T + tb * TB: dc * T + (tb + 1) * TB],
                        start=(dc == 0), stop=(dc == DC - 1))
                assert ST.inc(ins) == st_h[(gi, tb, fc)]

        def emit_B(gi, tb):
            par = gi % 2
            tensor.wait_ge(SS.h, ss_hev[(gi, tb, GF - 1)])
            if not (gi == 0 and tb == 0):
                pgi, ptb = (gi, tb - 1) if tb >= 1 else (gi - 1, NTB - 1)
                tensor.wait_ge(SV.h, sv_yev[(pgi, ptb)])
            for fc in range(GF):
                for tq in range(2):
                    for dh in range(2):
                        ins = nc.tensor.matmul(
                            ps_y[:, (tq * 2 + dh) * 512:(tq * 2 + dh + 1) * 512],
                            h_grp[tb % 2][:, fc * TB + tq * 128: fc * TB + tq * 128 + 128],
                            w2bf[par][:, fc * 1024 + dh * 512: fc * 1024 + dh * 512 + 512],
                            start=(fc == 0), stop=(fc == GF - 1))
            assert ST.inc(ins) == st_B[(gi, tb)]

        for gi in range(NGI):
            par = gi % 2
            tensor.wait_ge(WD[par].h, 32 * (gi // 2 + 1))
            emit_A(gi, 0)
            for tb in range(1, NTB):
                emit_A(gi, tb)
                emit_B(gi, tb - 1)
            emit_B(gi, NTB - 1)
        assert ST.n == st_total

    @block.scalar
    def _(scalar):
        for gi in range(NGI):
            e, g = divmod(gi, NG)
            for tb in range(NTB):
                for fc in range(GF):
                    scalar.wait_ge(ST.h, st_h[(gi, tb, fc)])
                    lin = gi * NTB + tb - 2
                    if lin >= 0:
                        pgi, ptb = divmod(lin, NTB)
                        scalar.wait_ge(ST.h, st_B[(pgi, ptb)])
                    ins = nc.scalar.activation(
                        h_grp[tb % 2][:, fc * TB:(fc + 1) * TB],
                        ps_h[fc % 2][:, 0:TB], AF.Relu,
                        bias=b1sb[:, e * 32 + g * GF + fc: e * 32 + g * GF + fc + 1],
                        scale=1.0)
                    assert SS.inc(ins) == ss_hev[(gi, tb, fc)]
        assert SS.n == ss_total

    @block.vector
    def _(vector):
        for gi in range(NGI):
            e, g = divmod(gi, NG)
            for tb in range(NTB):
                vector.wait_ge(ST.h, st_B[(gi, tb)])
                for tq in range(2):
                    ti = tb * 2 + tq
                    for dh in range(2):
                        ins = nc.vector.scalar_tensor_tensor(
                            acc[:, ti * D + dh * 512: ti * D + dh * 512 + 512],
                            ps_y[:, (tq * 2 + dh) * 512:(tq * 2 + dh + 1) * 512],
                            gates[:, ti * E + e: ti * E + e + 1],
                            acc[:, ti * D + dh * 512: ti * D + dh * 512 + 512],
                            op0=ALU.mult, op1=ALU.add)
                assert SV.inc(ins) == sv_yev[(gi, tb)]
        assert SV.n == sv_final

    @block.sync
    def _(sync):
        sync.wait_ge(SV.h, sv_final)
        SOUT.inc(sync.dma_start(out=OUT.rearrange("(i p) d -> p i d", p=128),
                                in_=acc.ap().rearrange("p (i d) -> p i d", i=NT)), 16)
        sync.wait_ge(SOUT.h, 16)

    esB.close()
    sems.close()
    es.close()
    return nc


def kernel(L0_attn_x, L0_attn_Q, L0_attn_K, L0_attn_VO, L0_ffn_Wr, L0_ffn_br,
           L0_ffn_W1, L0_ffn_b1, L0_ffn_W2, L0_ffn_b2, _dbg=False, _cores=8):
    x = np.ascontiguousarray(np.asarray(L0_attn_x, dtype=np.float32))
    common = {
        "q": np.ascontiguousarray(np.asarray(L0_attn_Q, np.float32)),
        "k": np.ascontiguousarray(np.asarray(L0_attn_K, np.float32)),
        "vo": np.ascontiguousarray(np.asarray(L0_attn_VO, np.float32)),
        "wr": np.ascontiguousarray(np.asarray(L0_ffn_Wr, np.float32)),
        "br": np.ascontiguousarray(np.asarray(L0_ffn_br, np.float32)).reshape(1, E),
        "w1": np.ascontiguousarray(np.asarray(L0_ffn_W1, np.float32)),
        "b1": np.ascontiguousarray(np.asarray(L0_ffn_b1, np.float32)),
        "w2": np.ascontiguousarray(np.asarray(L0_ffn_W2, np.float32)),
        "b2": np.ascontiguousarray(np.asarray(L0_ffn_b2, np.float32)),
        "id32": np.eye(128, dtype=np.float32),
    }
    nc = build(bass.Bass("TRN2", target_bir_lowering=False), dbg=_dbg)
    in_maps = [dict(common, x=np.ascontiguousarray(x[c])) for c in range(_cores)]
    res = run_bass_kernel_spmd(nc, in_maps, core_ids=list(range(_cores)))
    out = np.stack([np.asarray(res.results[c]["out"]) for c in range(_cores)], axis=0)
    if _dbg:
        return out, res
    return out


# version 07
# speedup vs baseline: 1.0818x; 1.0818x over previous
import math
import numpy as np
from contextlib import ExitStack

import concourse.bass as bass
import concourse.mybir as mybir
from concourse.bass_utils import run_bass_kernel_spmd

F32 = mybir.dt.float32
BF16 = mybir.dt.bfloat16
AF = mybir.ActivationFunctionType
ALU = mybir.AluOpType
AX = mybir.AxisListType

T = 2048
NT = 16
D = 1024
DC = 8
E = 8
F = 4096
NG = 4
GF = 8
TB = 256
NTB = 8
RANK = 64
N_ITER = 25
THRESH = 1e-9
EPS = 1e-5
SCALE = 1.0 / math.sqrt(D)
NGI = E * NG


class Ctr:
    def __init__(self, es, nc, name):
        self.h = es.enter_context(nc.semaphore(name))
        self.n = 0

    def inc(self, instr, k=1):
        instr.then_inc(self.h, k)
        self.n += k
        return self.n


def _finish_early(nc, block, OUT, acc, SV, SOUT, NT):
    @block.sync
    def _(sync):
        sync.wait_ge(SV.h, SV.n)
        SOUT.inc(sync.dma_start(out=OUT.rearrange("(i p) d -> p i d", p=128),
                                in_=acc.ap().rearrange("p (i d) -> p i d", i=NT)), 16)
        sync.wait_ge(SOUT.h, 16)
    return nc


def build(nc, dbg=False):
    PH = 99
    X = nc.dram_tensor("x", [T, D], F32, kind="ExternalInput").ap()
    Q = nc.dram_tensor("q", [RANK, D], F32, kind="ExternalInput").ap()
    KI = nc.dram_tensor("k", [RANK, D], F32, kind="ExternalInput").ap()
    VO = nc.dram_tensor("vo", [D, D], F32, kind="ExternalInput").ap()
    WR = nc.dram_tensor("wr", [D, E], F32, kind="ExternalInput").ap()
    BR = nc.dram_tensor("br", [1, E], F32, kind="ExternalInput").ap()
    W1 = nc.dram_tensor("w1", [NGI, 128, DC * 1024], F32, kind="ExternalInput").ap()
    B1 = nc.dram_tensor("b1", [E, F], F32, kind="ExternalInput").ap()
    W2 = nc.dram_tensor("w2", [NGI, 128, GF * 1024], F32, kind="ExternalInput").ap()
    B2 = nc.dram_tensor("b2", [E, D], F32, kind="ExternalInput").ap()
    ID32 = nc.dram_tensor("id32", [128, 128], F32, kind="ExternalInput").ap()
    OUT = nc.dram_tensor("out", [T, D], F32, kind="ExternalOutput").ap()
    if dbg:
        DBG_G = nc.dram_tensor("dbg_g", [128, NT * E], F32, kind="ExternalOutput").ap()
        DBG_A = nc.dram_tensor("dbg_a", [T, D], F32, kind="ExternalOutput").ap()
        DBG_XS = nc.dram_tensor("dbg_xs", [128, NT * E], F32, kind="ExternalOutput").ap()

    c = 0
    st_xnt = [0] * NT
    for i in range(NT):
        c += 1; st_xnt[i] = c
    c += 1; st_brbc = c
    st_logit = [0] * NT
    for i in range(NT):
        c += 1; st_logit[i] = c
    c += 1; st_qt = c
    c += 1; st_kt = c
    st_b1 = [0] * E
    for e in range(E):
        c += 1; st_b1[e] = c
    st_xq = [0] * 4
    for n in range(4):
        c += 1; st_xq[n] = c
    st_xk = [0] * 4
    for n in range(4):
        c += 1; st_xk[n] = c
    st_v = [0] * 32
    for k in range(32):
        c += 1; st_v[k] = c
    st_sc = [0] * NT; st_pt = [0] * NT; st_pv = [0] * NT
    c += 1; st_sc[0] = c
    for i in range(NT):
        c += 1; st_pt[i] = c
        if i + 1 < NT:
            c += 1; st_sc[i + 1] = c
        c += 1; st_pv[i] = c
    st_gT = [0] * NT
    for i in range(NT):
        c += 1; st_gT[i] = c
    st_gb2 = [0] * 32
    for k in range(32):
        c += 1; st_gb2[k] = c
    st_h = {}; st_B = {}; st_wfree1 = [0] * NGI; st_wfree2 = [0] * NGI
    for gi in range(NGI):
        for tb in range(NTB):
            for fc in range(GF):
                c += 1; st_h[(gi, tb, fc)] = c
            if tb >= 1:
                c += 1; st_B[(gi, tb - 1)] = c
        st_wfree1[gi] = st_h[(gi, NTB - 1, GF - 1)]
        c += 1; st_B[(gi, NTB - 1)] = c
        st_wfree2[gi] = c
    st_total = c

    c = 0
    c += 1; s_rstd = c
    ss_exp = [0] * NT
    for i in range(NT):
        c += 1; ss_exp[i] = c
    ss_hev = {}
    for gi in range(NGI):
        for tb in range(NTB):
            for fc in range(GF):
                c += 1; ss_hev[(gi, tb, fc)] = c
    ss_total = c

    c = 0
    c += 2; v_pre = c
    c += 1; v_negmean = c
    sv_xn_ts = [0] * NT; sv_xn_ev = [0] * NT
    xn_order = [("ts", 0), ("ts", 1)]
    for i in range(2, NT + 2):
        xn_order.append(("ev", i - 2))
        if i < NT:
            xn_order.append(("ts", i))
    for kind, idx in xn_order:
        c += 1
        if kind == "ts":
            sv_xn_ts[idx] = c
        else:
            sv_xn_ev[idx] = c
    c += 1; sv_brb = c
    sv_xs = [0] * NT
    for i in range(NT):
        c += 1; sv_xs[i] = c
    c += 1; sv_qk = c
    sv_b1ev = [0] * E
    for e in range(E):
        c += 1; sv_b1ev[e] = c
    sv_xqe = [0] * 4
    for n in range(4):
        c += 1; sv_xqe[n] = c
    sv_xke = [0] * 4
    for n in range(4):
        c += 1; sv_xke[n] = c
    sv_ve = [0] * 32
    for k in range(32):
        c += 1; sv_ve[k] = c
    sv_aux_done = c
    c += 1
    sv_gates = c
    sv_pte = [0] * NT; sv_oev = [0] * NT
    for i in range(NT):
        c += 1
        c += 1; sv_pte[i] = c
        c += 1; sv_oev[i] = c
    sv_gTev = [0] * NT
    for i in range(NT):
        c += 1; sv_gTev[i] = c
    sv_gb2 = [0] * 32
    for k in range(32):
        c += 1; sv_gb2[k] = c
    sv_all_A = c
    sv_yev = {}
    for gi in range(NGI):
        for tb in range(NTB):
            c += 1; sv_yev[(gi, tb)] = c
    sv_final = c

    es = ExitStack()
    sb = lambda n, s, d: es.enter_context(nc.sbuf_tensor(n, s, d))
    acc = sb("acc", [128, NT * D], F32)
    xnT = sb("xnT", [128, DC * T], BF16)
    gates = sb("gates", [128, NT * E], F32)
    b1sb = sb("b1sb", [128, E * 32], F32)
    b1raw = sb("b1raw", [32, E * 128], F32)
    ident_bf = sb("ident_bf", [128, 128], BF16)
    ident_f32 = sb("ident_f32", [128, 128], F32)
    wrbf = sb("wrbf", [128, DC * E], BF16)
    brpad = sb("brpad", [128, E], BF16)
    brb_half = sb("brb_half", [128, E], F32)
    e0row = sb("e0row", [128, 128], BF16)
    b2p = sb("b2p", [128, D], BF16)
    gTp = sb("gTp", [128, T], BF16)
    sums = sb("sums", [128, NT], F32)
    negmean = sb("negmean", [128, NT], F32)
    sumsq = sb("sumsq", [128, NT], F32)
    lnv = sb("lnv", [128, NT], F32)
    rstd = sb("rstd", [128, NT], F32)
    junk = sb("junk", [128, D], BF16)
    Xs = sb("Xs", [128, NT * E], F32)
    mx = sb("mx", [128, NT], F32)
    tauL = sb("tauL", [128, NT], F32)
    tauM = sb("tauM", [128, NT], F32)
    sL = sb("sL", [128, NT], F32)
    fLo = sb("fLo", [128, NT], F32)
    sM = sb("sM", [128, NT], F32)
    prod = sb("prod", [128, NT], F32)
    delta = sb("delta", [128, NT], F32)
    rbuf = sb("rbuf", [128, NT * E], F32)
    sqbuf = sb("sqbuf", [128, NT * E], F32)
    pbuf = sb("pbuf", [128, NT * E], F32)
    t1buf = sb("t1buf", [128, NT * E], F32)
    maskbuf = sb("maskbuf", [128, NT * E], F32)
    Zg = sb("Zg", [128, NT], F32)
    Zinv = sb("Zinv", [128, NT], F32)
    mxc4 = sb("mxc4", [128, NT * 4], F32)
    mxr = sb("mxr", [128, NT], F32)
    negmax = sb("negmax", [128, NT], F32)
    zp = sb("zp", [128, NT * 4], F32)
    Zcol = sb("Zcol", [128, NT], F32)
    Zvinv = sb("Zvinv", [128, NT], F32)
    zerocol = sb("zerocol", [128, 1], F32)
    epscol = sb("epscol", [128, 1], F32)
    ccol = sb("ccol", [128, 1], F32)

    sems = ExitStack()
    SY = Ctr(sems, nc, "SY")
    GP = Ctr(sems, nc, "GP")
    WD1 = [Ctr(sems, nc, "WDA0"), Ctr(sems, nc, "WDA1")]
    WD2 = [Ctr(sems, nc, "WDB0"), Ctr(sems, nc, "WDB1")]
    SOUT = Ctr(sems, nc, "SOUT")
    SV = Ctr(sems, nc, "SV")
    SS = Ctr(sems, nc, "SS")
    ST = Ctr(sems, nc, "ST")
    SD = Ctr(sems, nc, "SD")
    SB = Ctr(sems, nc, "SB")
    SB2 = Ctr(sems, nc, "SB2")

    block = es.enter_context(nc.Block())

    esA = ExitStack()
    sbA = lambda n, s, d: esA.enter_context(nc.sbuf_tensor(n, s, d))
    xnb = [sbA("xnb0", [128, D], BF16), sbA("xnb1", [128, D], BF16)]
    qbf = sbA("qbf", [RANK, D], BF16)
    kbf = sbA("kbf", [RANK, D], BF16)
    QTp = sbA("QTp", [128, DC * 128], BF16)
    KTp = sbA("KTp", [128, DC * 128], BF16)
    xqT = sbA("xqT", [128, T], BF16)
    xkT = sbA("xkT", [128, T], BF16)
    v_sb = sbA("v_sb", [128, NT * D], BF16)
    P_sb = sbA("P_sb", [128, T], BF16)
    PT_sb = sbA("PT_sb", [128, T], BF16)
    vobf = sbA("vobf", [128, DC * D], BF16)

    esP1 = ExitStack()
    psA = lambda n, s, d: esP1.enter_context(nc.psum_tensor(n, s, d))
    pt_xn = psA("pt_xn", [128, 1024], BF16)
    pmisc = psA("pmisc", [128, 1024], BF16)
    pmisc32 = psA("pmisc32", [128, 512], F32)
    ps_big = psA("ps_big", [128, 2048], F32)

    @block.sync
    def _(sync):
        SY.inc(sync.dma_start(out=acc.ap().rearrange("p (i d) -> p i d", i=NT),
                              in_=X.rearrange("(i p) d -> p i d", p=128)), 16)
        SY.inc(sync.dma_start(out=b1raw.ap().rearrange("j (e c) -> j e c", e=E),
                              in_=B1.rearrange("e (j c) -> j e c", c=128)), 16)
        SY.inc(sync.dma_start(out=ident_f32[:, :], in_=ID32[:, :]), 16)
    sy_setup = SY.n

    @block.vector
    def _(vector):
        nc.vector.memset(zerocol[:, :], 0.0)
        nc.vector.memset(epscol[:, :], EPS)
        nc.vector.memset(ccol[:, :], -32.0)
        SV.inc(nc.vector.memset(e0row[:, :], 0.0))
        vector.wait_ge(SV.h, SV.n)
        nc.vector.memset(e0row[0:1, :], 1.0)
        nc.vector.memset(brpad[:, :], 0.0)
        nc.vector.memset(b2p[:, :], 0.0)
        nc.vector.memset(gTp[:, :], 0.0)
        nc.vector.memset(QTp[:, :], 0.0)
        assert SV.inc(nc.vector.memset(KTp[:, :], 0.0)) == v_pre

    @block.gpsimd
    def _(gpsimd):
        GP.inc(gpsimd.dma_start(out=ident_bf[:, :], in_=ID32[:, :]), 16)
        GP.inc(gpsimd.dma_start(out=qbf[:, :], in_=Q[:, :]), 16)
        GP.inc(gpsimd.dma_start(out=kbf[:, :], in_=KI[:, :]), 16)
        GP.inc(gpsimd.dma_start(out=vobf.ap().rearrange("p (dc e) -> p dc e", dc=DC),
                                in_=VO.rearrange("(dc p) e -> p dc e", p=128)), 16)
        GP.inc(gpsimd.dma_start(out=wrbf.ap().rearrange("p (dc e) -> p dc e", dc=DC),
                                in_=WR.rearrange("(dc p) e -> p dc e", p=128)), 16)
        gpsimd.wait_ge(SV.h, v_pre)
        GP.inc(gpsimd.dma_start(out=brpad[0:1, :], in_=BR[:, :]), 16)
        GP.inc(gpsimd.dma_start(out=b2p[0:8, :], in_=B2[:, :]), 16)
    gp_setup = GP.n

    @block.vector
    def _(vector):
        vector.wait_ge(SY.h, sy_setup)
        for i in range(NT):
            ins = nc.vector.reduce_sum(sums[:, i:i + 1], acc[:, i * D:(i + 1) * D], axis=AX.X)
        SB.inc(ins); vector.wait_ge(SB.h, SB.n)
        for i in range(NT):
            ins = nc.vector.tensor_scalar(negmean[:, i:i + 1], sums[:, i:i + 1],
                                          -1.0 / D, None, op0=ALU.mult)
        assert SV.inc(ins) == v_negmean

    @block.scalar
    def _(scalar):
        scalar.wait_ge(SV.h, v_negmean)
        for i in range(NT):
            ins = nc.scalar.activation(junk[:, :], acc[:, i * D:(i + 1) * D], AF.Square,
                                       bias=negmean[:, i:i + 1], scale=1.0,
                                       accum_out=sumsq[:, i:i + 1])
            SB2.inc(ins); scalar.wait_ge(SB2.h, SB2.n)
        for i in range(NT):
            ins = nc.scalar.activation(lnv[:, i:i + 1], sumsq[:, i:i + 1], AF.Ln,
                                       bias=epscol[:, :], scale=1.0 / D)
        SB2.inc(ins); scalar.wait_ge(SB2.h, SB2.n)
        for i in range(NT):
            ins = nc.scalar.activation(rstd[:, i:i + 1], lnv[:, i:i + 1], AF.Exp,
                                       bias=zerocol[:, :], scale=-0.5)
        assert SS.inc(ins) == s_rstd

    @block.vector
    def _(vector):
        vector.wait_ge(SS.h, s_rstd)
        for kind, i in xn_order:
            if kind == "ts":
                if i >= 2:
                    vector.wait_ge(ST.h, st_xnt[i - 2])
                ins = nc.vector.tensor_scalar(xnb[i % 2][:, :], acc[:, i * D:(i + 1) * D],
                                              negmean[:, i:i + 1], rstd[:, i:i + 1],
                                              op0=ALU.add, op1=ALU.mult)
                assert SV.inc(ins) == sv_xn_ts[i]
            else:
                vector.wait_ge(ST.h, st_xnt[i])
                for j in range(DC):
                    ins = nc.vector.tensor_copy(
                        xnT[:, j * T + i * 128: j * T + i * 128 + 128],
                        pt_xn[:, j * 128:(j + 1) * 128])
                assert SV.inc(ins) == sv_xn_ev[i]

    @block.tensor
    def _(tensor):
        tensor.wait_ge(GP.h, gp_setup)
        for i in range(NT):
            tensor.wait_ge(SV.h, sv_xn_ts[i])
            if i >= 1:
                tensor.wait_ge(SV.h, sv_xn_ev[i - 1])
            for j in range(DC):
                ins = nc.tensor.transpose(pt_xn[:, j * 128:(j + 1) * 128],
                                          xnb[i % 2][:, j * 128:(j + 1) * 128],
                                          ident_bf[:, :])
            assert ST.inc(ins) == st_xnt[i]
        ins = nc.tensor.matmul(pmisc32[:, 448:448 + E], e0row[:, :], brpad[:, :],
                               start=True, stop=True)
        assert ST.inc(ins) == st_brbc
        for i in range(NT):
            tensor.wait_ge(SV.h, sv_xn_ev[i])
            if i >= 4:
                tensor.wait_ge(SV.h, sv_xs[i - 4])
            base = (i % 4) * 512
            for dc in range(DC):
                ins = nc.tensor.matmul(ps_big[:, base:base + E],
                                       xnT[:, dc * T + i * 128: dc * T + i * 128 + 128],
                                       wrbf[:, dc * E:(dc + 1) * E],
                                       start=(dc == 0), stop=(dc == DC - 1))
            assert ST.inc(ins) == st_logit[i]
        for dc in range(DC):
            ins = nc.tensor.transpose(pmisc[:, dc * 64:(dc + 1) * 64],
                                      qbf[:, dc * 128:(dc + 1) * 128],
                                      ident_bf[0:64, 0:64])
        assert ST.inc(ins) == st_qt
        for dc in range(DC):
            ins = nc.tensor.transpose(pmisc[:, 512 + dc * 64: 512 + (dc + 1) * 64],
                                      kbf[:, dc * 128:(dc + 1) * 128],
                                      ident_bf[0:64, 0:64])
        assert ST.inc(ins) == st_kt
        tensor.wait_ge(SY.h, sy_setup)
        tensor.wait_ge(SV.h, sv_brb)
        for e in range(E):
            if e >= 1:
                tensor.wait_ge(SV.h, sv_b1ev[e - 1])
            ins = nc.tensor.transpose(pmisc32[:, (e % 4) * 128:(e % 4) * 128 + 32],
                                      b1raw[:, e * 128:(e + 1) * 128],
                                      ident_f32[0:32, 0:32])
            assert ST.inc(ins) == st_b1[e]

    @block.vector
    def _(vector):
        vector.wait_ge(ST.h, st_brbc)
        ins = nc.vector.tensor_scalar(brb_half[:, :], pmisc32[:, 448:448 + E],
                                      0.5, None, op0=ALU.mult)
        assert SV.inc(ins) == sv_brb
        vector.wait_ge(SV.h, sv_brb)
        for i in range(NT):
            vector.wait_ge(ST.h, st_logit[i])
            base = (i % 4) * 512
            ins = nc.vector.scalar_tensor_tensor(Xs[:, i * E:(i + 1) * E],
                                                 ps_big[:, base:base + E],
                                                 0.5, brb_half[:, :],
                                                 op0=ALU.mult, op1=ALU.add)
            assert SV.inc(ins) == sv_xs[i]
        vector.wait_ge(ST.h, st_kt)
        vector.wait_ge(SV.h, v_pre)
        for dc in range(DC):
            nc.vector.tensor_copy(QTp[:, dc * 128: dc * 128 + 64],
                                  pmisc[:, dc * 64:(dc + 1) * 64])
        for dc in range(DC):
            ins = nc.vector.tensor_copy(KTp[:, dc * 128: dc * 128 + 64],
                                        pmisc[:, 512 + dc * 64: 512 + (dc + 1) * 64])
        assert SV.inc(ins) == sv_qk
        for e in range(E):
            vector.wait_ge(ST.h, st_b1[e])
            ins = nc.vector.tensor_copy(b1sb[:, e * 32:(e + 1) * 32],
                                        pmisc32[:, (e % 4) * 128:(e % 4) * 128 + 32])
            assert SV.inc(ins) == sv_b1ev[e]

    if PH < 4:
        return _finish_early(nc, block, OUT, acc, SV, SOUT, NT)
    @block.tensor
    def _(tensor):
        tensor.wait_ge(SV.h, sv_xs[NT - 1])
        tensor.wait_ge(SV.h, sv_qk)
        for n in range(4):
            for dc in range(DC):
                ins = nc.tensor.matmul(ps_big[:, n * 512:(n + 1) * 512],
                                       QTp[:, dc * 128:(dc + 1) * 128],
                                       xnT[:, dc * T + n * 512: dc * T + (n + 1) * 512],
                                       start=(dc == 0), stop=(dc == DC - 1))
            assert ST.inc(ins) == st_xq[n]
        for n in range(4):
            tensor.wait_ge(SV.h, sv_xqe[n])
            for dc in range(DC):
                ins = nc.tensor.matmul(ps_big[:, n * 512:(n + 1) * 512],
                                       KTp[:, dc * 128:(dc + 1) * 128],
                                       xnT[:, dc * T + n * 512: dc * T + (n + 1) * 512],
                                       start=(dc == 0), stop=(dc == DC - 1))
            assert ST.inc(ins) == st_xk[n]
        for k in range(32):
            ut, eh = divmod(k, 2)
            slot = k % 4
            tensor.wait_ge(SV.h, sv_xke[slot] if k < 4 else sv_ve[k - 4])
            for dc in range(DC):
                ins = nc.tensor.matmul(
                    ps_big[:, slot * 512:(slot + 1) * 512],
                    xnT[:, dc * T + ut * 128: dc * T + ut * 128 + 128],
                    vobf[:, dc * D + eh * 512: dc * D + eh * 512 + 512],
                    start=(dc == 0), stop=(dc == DC - 1))
            assert ST.inc(ins) == st_v[k]

    @block.vector
    def _(vector):
        for n in range(4):
            vector.wait_ge(ST.h, st_xq[n])
            ins = nc.vector.tensor_scalar(xqT[:, n * 512:(n + 1) * 512],
                                          ps_big[:, n * 512:(n + 1) * 512],
                                          SCALE, None, op0=ALU.mult)
            assert SV.inc(ins) == sv_xqe[n]
        for n in range(4):
            vector.wait_ge(ST.h, st_xk[n])
            ins = nc.vector.tensor_copy(xkT[:, n * 512:(n + 1) * 512],
                                        ps_big[:, n * 512:(n + 1) * 512])
            assert SV.inc(ins) == sv_xke[n]
        for k in range(32):
            ut, eh = divmod(k, 2)
            slot = k % 4
            vector.wait_ge(ST.h, st_v[k])
            ins = nc.vector.tensor_copy(
                v_sb[:, ut * D + eh * 512: ut * D + eh * 512 + 512],
                ps_big[:, slot * 512:(slot + 1) * 512])
            assert SV.inc(ins) == sv_ve[k]
        assert SV.n == sv_aux_done

    if PH < 5:
        return _finish_early(nc, block, OUT, acc, SV, SOUT, NT)
    def vbar(vector, ins):
        SV.inc(ins)
        vector.wait_ge(SV.h, SV.n)

    @block.vector
    def _(vector):
        vector.wait_ge(SV.h, sv_xs[NT - 1])
        HT = NT // 2
        HW = HT * E
        X3 = [Xs[:, h * HW:(h + 1) * HW].rearrange("p (i e) -> p i e", e=E)
              for h in (0, 1)]
        D3 = [rbuf[:, h * HW:(h + 1) * HW].rearrange("p (i e) -> p i e", e=E)
              for h in (0, 1)]
        SQ3 = [sqbuf[:, h * HW:(h + 1) * HW].rearrange("p (i e) -> p i e", e=E)
               for h in (0, 1)]
        P3 = [pbuf[:, h * HW:(h + 1) * HW].rearrange("p (i e) -> p i e", e=E)
              for h in (0, 1)]
        T3 = [t1buf[:, h * HW:(h + 1) * HW].rearrange("p (i e) -> p i e", e=E)
              for h in (0, 1)]
        G3 = [gates[:, h * HW:(h + 1) * HW].rearrange("p (i e) -> p i e", e=E)
              for h in (0, 1)]
        col = lambda buf, h: buf[:, h * HT:(h + 1) * HT]
        c3 = lambda buf, h: buf[:, h * HT:(h + 1) * HT].unsqueeze(-1)
        bc = lambda buf, h: buf[:, h * HT:(h + 1) * HT].unsqueeze(-1).broadcast_to([128, HT, E])
        for h in (0, 1):
            nc.vector.reduce_max(c3(mx, h), X3[h], axis=AX.X)
        for h in (0, 1):
            nc.vector.tensor_scalar(col(tauL, h), col(mx, h), -1.0, None, op0=ALU.add)
        for h in (0, 1):
            nc.vector.tensor_tensor(D3[h], X3[h], bc(tauL, h), op=ALU.subtract)
        for h in (0, 1):
            nc.vector.scalar_tensor_tensor(SQ3[h], D3[h], 0.0, D3[h],
                                           op0=ALU.max, op1=ALU.mult)
        for h in (0, 1):
            nc.vector.reduce_sum(c3(sL, h), SQ3[h], axis=AX.X)
        for h in (0, 1):
            nc.vector.tensor_scalar(col(fLo, h), col(sL, h), -1.0, None, op0=ALU.add)
        dm = 1.0 - (1.0 / E) ** 0.5
        for it in range(N_ITER):
            dm = dm * 0.5
            for h in (0, 1):
                nc.vector.tensor_scalar(col(tauM, h), col(tauL, h), dm, None, op0=ALU.add)
            for h in (0, 1):
                nc.vector.tensor_tensor(D3[h], X3[h], bc(tauM, h), op=ALU.subtract)
            for h in (0, 1):
                nc.vector.scalar_tensor_tensor(SQ3[h], D3[h], 0.0, D3[h],
                                               op0=ALU.max, op1=ALU.mult)
            for h in (0, 1):
                nc.vector.reduce_sum(c3(sM, h), SQ3[h], axis=AX.X)
            for h in (0, 1):
                nc.vector.scalar_tensor_tensor(col(prod, h), col(sM, h), -1.0,
                                               col(fLo, h), op0=ALU.add, op1=ALU.mult)
            for h in (0, 1):
                nc.vector.tensor_scalar(col(delta, h), col(prod, h), 0.0, dm,
                                        op0=ALU.is_ge, op1=ALU.mult)
            for h in (0, 1):
                nc.vector.tensor_tensor(col(tauL, h), col(tauL, h), col(delta, h),
                                        op=ALU.add)
        for h in (0, 1):
            nc.vector.tensor_tensor(D3[h], X3[h], bc(tauM, h), op=ALU.subtract)
        for h in (0, 1):
            nc.vector.scalar_tensor_tensor(P3[h], D3[h], 0.0, D3[h],
                                           op0=ALU.max, op1=ALU.mult)
        for h in (0, 1):
            nc.vector.reduce_sum(c3(Zg, h), P3[h], axis=AX.X)
        for h in (0, 1):
            nc.vector.reciprocal(col(Zinv, h), col(Zg, h))
        for h in (0, 1):
            nc.vector.tensor_tensor(T3[h], P3[h], bc(Zinv, h), op=ALU.mult)
        for h in (0, 1):
            ins = nc.vector.scalar_tensor_tensor(G3[h], T3[h], THRESH, T3[h],
                                                 op0=ALU.is_gt, op1=ALU.mult)
        assert SV.inc(ins) == sv_gates

    if PH < 6:
        return _finish_early(nc, block, OUT, acc, SV, SOUT, NT)
    esP1.close()
    esP2 = ExitStack()
    psB = lambda n, s, d: esP2.enter_context(nc.psum_tensor(n, s, d))
    ps_s = psB("ps_s", [128, 2048], F32)
    ps_pt = psB("ps_pt", [128, 2048], BF16)
    ps_o = psB("ps_o", [128, 1024], F32)

    @block.tensor
    def _(tensor):
        tensor.wait_ge(SV.h, sv_aux_done)

        def emit_scores(i):
            if i >= 1:
                tensor.wait_ge(SS.h, ss_exp[i - 1])
            for n in range(4):
                ins = nc.tensor.matmul(ps_s[:, n * 512:(n + 1) * 512],
                                       xqT[:, i * 128:(i + 1) * 128],
                                       xkT[:, n * 512:(n + 1) * 512],
                                       start=True, stop=True)
            assert ST.inc(ins) == st_sc[i]

        emit_scores(0)
        for i in range(NT):
            tensor.wait_ge(SS.h, ss_exp[i])
            if i >= 1:
                tensor.wait_ge(SV.h, sv_pte[i - 1])
            for j in range(NT):
                ins = nc.tensor.transpose(ps_pt[:, j * 128:(j + 1) * 128],
                                          P_sb[:, j * 128:(j + 1) * 128],
                                          ident_bf[:, :])
            assert ST.inc(ins) == st_pt[i]
            if i + 1 < NT:
                emit_scores(i + 1)
            tensor.wait_ge(SV.h, sv_pte[i])
            if i >= 1:
                tensor.wait_ge(SV.h, sv_oev[i - 1])
            for eh in range(2):
                for uc in range(NT):
                    ins = nc.tensor.matmul(
                        ps_o[:, eh * 512:(eh + 1) * 512],
                        PT_sb[:, uc * 128:(uc + 1) * 128],
                        v_sb[:, uc * D + eh * 512: uc * D + eh * 512 + 512],
                        start=(uc == 0), stop=(uc == NT - 1))
            assert ST.inc(ins) == st_pv[i]

    @block.scalar
    def _(scalar):
        scalar.wait_ge(SV.h, v_pre)
        for i in range(NT):
            scalar.wait_ge(ST.h, st_sc[i])
            if i >= 1:
                scalar.wait_ge(ST.h, st_pt[i - 1])
            for n in range(4):
                ins = nc.scalar.activation(P_sb[:, n * 512:(n + 1) * 512],
                                           ps_s[:, n * 512:(n + 1) * 512], AF.Exp,
                                           bias=ccol[:, :], scale=1.0,
                                           accum_out=zp[:, i * 4 + n: i * 4 + n + 1])
            assert SS.inc(ins) == ss_exp[i]

    @block.vector
    def _(vector):
        for i in range(NT):
            vector.wait_ge(SS.h, ss_exp[i])
            ins = nc.vector.reduce_sum(Zcol[:, i:i + 1], zp[:, i * 4:(i + 1) * 4],
                                       axis=AX.X)
            vbar(vector, ins)
            ins = nc.vector.reciprocal(Zvinv[:, i:i + 1], Zcol[:, i:i + 1])
            SB.inc(ins); vector.wait_ge(SB.h, SB.n)
            vector.wait_ge(ST.h, st_pt[i])
            ins = nc.vector.tensor_copy(PT_sb[:, :], ps_pt[:, :])
            assert SV.inc(ins) == sv_pte[i]
            vector.wait_ge(ST.h, st_pv[i])
            for eh in range(2):
                ins = nc.vector.scalar_tensor_tensor(
                    acc[:, i * D + eh * 512: i * D + eh * 512 + 512],
                    ps_o[:, eh * 512:(eh + 1) * 512],
                    Zvinv[:, i:i + 1],
                    acc[:, i * D + eh * 512: i * D + eh * 512 + 512],
                    op0=ALU.mult, op1=ALU.add)
            assert SV.inc(ins) == sv_oev[i]

    if PH < 7:
        return _finish_early(nc, block, OUT, acc, SV, SOUT, NT)
    @block.tensor
    def _(tensor):
        tensor.wait_ge(SV.h, sv_gates)
        tensor.wait_ge(SS.h, ss_exp[NT - 1])
        for i in range(NT):
            if i >= 4:
                tensor.wait_ge(SV.h, sv_gTev[i - 4])
            ins = nc.tensor.transpose(ps_s[0:8, (i % 4) * 512:(i % 4) * 512 + 128],
                                      gates[:, i * E:(i + 1) * E], ident_f32[:, :])
            assert ST.inc(ins) == st_gT[i]
        for k in range(32):
            ti, dh = divmod(k, 2)
            tensor.wait_ge(SV.h, sv_gTev[ti])
            if k < 2:
                tensor.wait_ge(SV.h, sv_oev[NT - 1])
            else:
                tensor.wait_ge(SV.h, sv_gb2[k - 2])
            ins = nc.tensor.matmul(ps_o[:, dh * 512:(dh + 1) * 512],
                                   gTp[:, ti * 128:(ti + 1) * 128],
                                   b2p[:, dh * 512:(dh + 1) * 512],
                                   start=True, stop=True)
            assert ST.inc(ins) == st_gb2[k]

    @block.vector
    def _(vector):
        vector.wait_ge(SV.h, v_pre)
        for i in range(NT):
            vector.wait_ge(ST.h, st_gT[i])
            ins = nc.vector.tensor_copy(gTp[0:8, i * 128:(i + 1) * 128],
                                        ps_s[0:8, (i % 4) * 512:(i % 4) * 512 + 128])
            assert SV.inc(ins) == sv_gTev[i]
        for k in range(32):
            ti, dh = divmod(k, 2)
            vector.wait_ge(ST.h, st_gb2[k])
            ins = nc.vector.tensor_tensor(
                acc[:, ti * D + dh * 512: ti * D + dh * 512 + 512],
                acc[:, ti * D + dh * 512: ti * D + dh * 512 + 512],
                ps_o[:, dh * 512:(dh + 1) * 512], op=ALU.add)
            assert SV.inc(ins) == sv_gb2[k]
        assert SV.n == sv_all_A

    if dbg:
        @block.sync
        def _(sync):
            sync.wait_ge(SV.h, sv_all_A)
            SD.inc(sync.dma_start(out=DBG_G[:, :], in_=gates[:, :]), 16)
            SD.inc(sync.dma_start(out=DBG_XS[:, :], in_=Xs[:, :]), 16)
            SD.inc(sync.dma_start(out=DBG_A.rearrange("(i p) d -> p i d", p=128),
                                  in_=acc.ap().rearrange("p (i d) -> p i d", i=NT)), 16)
    sd_dbg = SD.n

    if PH < 8:
        return _finish_early(nc, block, OUT, acc, SV, SOUT, NT)
    esP2.close()
    esA.close()
    esB = ExitStack()
    sbB = lambda n, s, d: esB.enter_context(nc.sbuf_tensor(n, s, d))
    psC = lambda n, s, d: esB.enter_context(nc.psum_tensor(n, s, d))
    w1bf = [sbB("w1bf0", [128, GF * D], BF16), sbB("w1bf1", [128, GF * D], BF16)]
    w2bf = [sbB("w2bf0", [128, GF * D], BF16), sbB("w2bf1", [128, GF * D], BF16)]
    h_grp = [sbB("hg0", [128, GF * TB], BF16), sbB("hg1", [128, GF * TB], BF16)]
    ps_h = [psC("ps_h0", [128, 512], F32), psC("ps_h1", [128, 512], F32)]
    ps_y = psC("ps_y", [128, 2048], F32)

    @block.gpsimd
    def _(gpsimd):
        for gi in range(NGI):
            par = gi % 2
            if gi < 2:
                gpsimd.wait_ge(SV.h, sv_all_A)
                if dbg:
                    gpsimd.wait_ge(SD.h, sd_dbg)
            else:
                gpsimd.wait_ge(ST.h, st_wfree1[gi - 2])
            WD1[par].inc(gpsimd.dma_start(out=w1bf[par].ap(), in_=W1[gi, :, :]), 16)
            if gi >= 2:
                gpsimd.wait_ge(ST.h, st_wfree2[gi - 2])
            WD2[par].inc(gpsimd.dma_start(out=w2bf[par].ap(), in_=W2[gi, :, :]), 16)

    @block.tensor
    def _(tensor):
        tensor.wait_ge(SV.h, sv_all_A)

        def emit_A(gi, tb):
            par = gi % 2
            for fc in range(GF):
                lin = (gi * NTB + tb) * GF + fc - 2
                if lin >= 0:
                    pgi, rem = divmod(lin, NTB * GF)
                    ptb, pfc = divmod(rem, GF)
                    tensor.wait_ge(SS.h, ss_hev[(pgi, ptb, pfc)])
                for dc in range(DC):
                    ins = nc.tensor.matmul(
                        ps_h[fc % 2][:, 0:TB],
                        w1bf[par][:, dc * 1024 + fc * 128: dc * 1024 + fc * 128 + 128],
                        xnT[:, dc * T + tb * TB: dc * T + (tb + 1) * TB],
                        start=(dc == 0), stop=(dc == DC - 1))
                assert ST.inc(ins) == st_h[(gi, tb, fc)]

        def emit_B(gi, tb):
            par = gi % 2
            tensor.wait_ge(SS.h, ss_hev[(gi, tb, GF - 1)])
            if not (gi == 0 and tb == 0):
                pgi, ptb = (gi, tb - 1) if tb >= 1 else (gi - 1, NTB - 1)
                tensor.wait_ge(SV.h, sv_yev[(pgi, ptb)])
            for fc in range(GF):
                for tq in range(2):
                    for dh in range(2):
                        ins = nc.tensor.matmul(
                            ps_y[:, (tq * 2 + dh) * 512:(tq * 2 + dh + 1) * 512],
                            h_grp[tb % 2][:, fc * TB + tq * 128: fc * TB + tq * 128 + 128],
                            w2bf[par][:, fc * 1024 + dh * 512: fc * 1024 + dh * 512 + 512],
                            start=(fc == 0), stop=(fc == GF - 1))
            assert ST.inc(ins) == st_B[(gi, tb)]

        for gi in range(NGI):
            par = gi % 2
            tensor.wait_ge(WD1[par].h, 16 * (gi // 2 + 1))
            emit_A(gi, 0)
            for tb in range(1, NTB):
                emit_A(gi, tb)
                if tb == 1:
                    tensor.wait_ge(WD2[par].h, 16 * (gi // 2 + 1))
                emit_B(gi, tb - 1)
            emit_B(gi, NTB - 1)
        assert ST.n == st_total

    @block.scalar
    def _(scalar):
        for gi in range(NGI):
            e, g = divmod(gi, NG)
            for tb in range(NTB):
                for fc in range(GF):
                    scalar.wait_ge(ST.h, st_h[(gi, tb, fc)])
                    lin = gi * NTB + tb - 2
                    if lin >= 0:
                        pgi, ptb = divmod(lin, NTB)
                        scalar.wait_ge(ST.h, st_B[(pgi, ptb)])
                    ins = nc.scalar.activation(
                        h_grp[tb % 2][:, fc * TB:(fc + 1) * TB],
                        ps_h[fc % 2][:, 0:TB], AF.Relu,
                        bias=b1sb[:, e * 32 + g * GF + fc: e * 32 + g * GF + fc + 1],
                        scale=1.0)
                    assert SS.inc(ins) == ss_hev[(gi, tb, fc)]
        assert SS.n == ss_total

    @block.vector
    def _(vector):
        for gi in range(NGI):
            e, g = divmod(gi, NG)
            for tb in range(NTB):
                vector.wait_ge(ST.h, st_B[(gi, tb)])
                for tq in range(2):
                    ti = tb * 2 + tq
                    for dh in range(2):
                        ins = nc.vector.scalar_tensor_tensor(
                            acc[:, ti * D + dh * 512: ti * D + dh * 512 + 512],
                            ps_y[:, (tq * 2 + dh) * 512:(tq * 2 + dh + 1) * 512],
                            gates[:, ti * E + e: ti * E + e + 1],
                            acc[:, ti * D + dh * 512: ti * D + dh * 512 + 512],
                            op0=ALU.mult, op1=ALU.add)
                assert SV.inc(ins) == sv_yev[(gi, tb)]
        assert SV.n == sv_final

    @block.sync
    def _(sync):
        for tb in range(NTB):
            sync.wait_ge(SV.h, sv_yev[(NGI - 1, tb)])
            SOUT.inc(sync.dma_start(
                out=OUT[tb * TB:(tb + 1) * TB, :].rearrange("(i p) d -> p i d", p=128),
                in_=acc[:, 2 * tb * D:(2 * tb + 2) * D].rearrange("p (i d) -> p i d", i=2)), 16)
        sync.wait_ge(SOUT.h, 16 * NTB)

    esB.close()
    sems.close()
    es.close()
    return nc


def kernel(L0_attn_x, L0_attn_Q, L0_attn_K, L0_attn_VO, L0_ffn_Wr, L0_ffn_br,
           L0_ffn_W1, L0_ffn_b1, L0_ffn_W2, L0_ffn_b2, _dbg=False, _cores=8):
    x = np.ascontiguousarray(np.asarray(L0_attn_x, dtype=np.float32))
    common = {
        "q": np.ascontiguousarray(np.asarray(L0_attn_Q, np.float32)),
        "k": np.ascontiguousarray(np.asarray(L0_attn_K, np.float32)),
        "vo": np.ascontiguousarray(np.asarray(L0_attn_VO, np.float32)),
        "wr": np.ascontiguousarray(np.asarray(L0_ffn_Wr, np.float32)),
        "br": np.ascontiguousarray(np.asarray(L0_ffn_br, np.float32)).reshape(1, E),
        "w1": np.ascontiguousarray(
            np.asarray(L0_ffn_W1, np.float32).reshape(E, DC, 128, NG, 1024)
            .transpose(0, 3, 2, 1, 4)).reshape(NGI, 128, DC * 1024),
        "b1": np.ascontiguousarray(np.asarray(L0_ffn_b1, np.float32)),
        "w2": np.ascontiguousarray(
            np.asarray(L0_ffn_W2, np.float32).reshape(E, NG, GF, 128, 1024)
            .transpose(0, 1, 3, 2, 4)).reshape(NGI, 128, GF * 1024),
        "b2": np.ascontiguousarray(np.asarray(L0_ffn_b2, np.float32)),
        "id32": np.eye(128, dtype=np.float32),
    }
    nc = build(bass.Bass("TRN2", target_bir_lowering=False), dbg=_dbg)
    in_maps = [dict(common, x=np.ascontiguousarray(x[c])) for c in range(_cores)]
    res = run_bass_kernel_spmd(nc, in_maps, core_ids=list(range(_cores)))
    out = np.stack([np.asarray(res.results[c]["out"]) for c in range(_cores)], axis=0)
    if _dbg:
        return out, res
    return out


# version 08
# speedup vs baseline: 1.0846x; 1.0025x over previous
import math
import numpy as np
from contextlib import ExitStack

import concourse.bass as bass
import concourse.mybir as mybir
from concourse.bass_utils import run_bass_kernel_spmd

F32 = mybir.dt.float32
BF16 = mybir.dt.bfloat16
AF = mybir.ActivationFunctionType
ALU = mybir.AluOpType
AX = mybir.AxisListType

T = 2048
NT = 16
D = 1024
DC = 8
E = 8
F = 4096
NG = 4
GF = 8
TB = 256
NTB = 8
RANK = 64
N_ITER = 25
THRESH = 1e-9
EPS = 1e-5
SCALE = 1.0 / math.sqrt(D)
NGI = E * NG


class Ctr:
    def __init__(self, es, nc, name):
        self.h = es.enter_context(nc.semaphore(name))
        self.n = 0

    def inc(self, instr, k=1):
        instr.then_inc(self.h, k)
        self.n += k
        return self.n


def _finish_early(nc, block, OUT, acc, SV, SOUT, NT):
    @block.sync
    def _(sync):
        sync.wait_ge(SV.h, SV.n)
        SOUT.inc(sync.dma_start(out=OUT.rearrange("(i p) d -> p i d", p=128),
                                in_=acc.ap().rearrange("p (i d) -> p i d", i=NT)), 16)
        sync.wait_ge(SOUT.h, 16)
    return nc


def build(nc, dbg=False):
    PH = 99
    X = nc.dram_tensor("x", [T, D], F32, kind="ExternalInput").ap()
    Q = nc.dram_tensor("q", [RANK, D], F32, kind="ExternalInput").ap()
    KI = nc.dram_tensor("k", [RANK, D], F32, kind="ExternalInput").ap()
    VO = nc.dram_tensor("vo", [D, D], F32, kind="ExternalInput").ap()
    WR = nc.dram_tensor("wr", [D, E], F32, kind="ExternalInput").ap()
    BR = nc.dram_tensor("br", [1, E], F32, kind="ExternalInput").ap()
    W1 = nc.dram_tensor("w1", [NGI, 128, DC * 1024], F32, kind="ExternalInput").ap()
    B1 = nc.dram_tensor("b1", [E, F], F32, kind="ExternalInput").ap()
    W2 = nc.dram_tensor("w2", [NGI, 128, GF * 1024], F32, kind="ExternalInput").ap()
    B2 = nc.dram_tensor("b2", [E, D], F32, kind="ExternalInput").ap()
    ID32 = nc.dram_tensor("id32", [128, 128], F32, kind="ExternalInput").ap()
    OUT = nc.dram_tensor("out", [T, D], F32, kind="ExternalOutput").ap()
    if dbg:
        DBG_G = nc.dram_tensor("dbg_g", [128, NT * E], F32, kind="ExternalOutput").ap()
        DBG_A = nc.dram_tensor("dbg_a", [T, D], F32, kind="ExternalOutput").ap()
        DBG_XS = nc.dram_tensor("dbg_xs", [128, NT * E], F32, kind="ExternalOutput").ap()

    c = 0
    st_xnt = [0] * NT
    for i in range(NT):
        c += 1; st_xnt[i] = c
    c += 1; st_brbc = c
    st_logit = [0] * NT
    for i in range(NT):
        c += 1; st_logit[i] = c
    c += 1; st_qt = c
    c += 1; st_kt = c
    st_b1 = [0] * E
    for e in range(E):
        c += 1; st_b1[e] = c
    st_xq = [0] * 4
    for n in range(4):
        c += 1; st_xq[n] = c
    st_xk = [0] * 4
    for n in range(4):
        c += 1; st_xk[n] = c
    st_v = [0] * 32
    for k in range(32):
        c += 1; st_v[k] = c
    st_sc = [0] * NT; st_pv = [0] * NT
    c += 1; st_sc[0] = c
    for i in range(NT):
        if i + 1 < NT:
            c += 1; st_sc[i + 1] = c
        c += 1; st_pv[i] = c
    st_gT = [0] * NT
    for i in range(NT):
        c += 1; st_gT[i] = c
    st_gb2 = [0] * 32
    for k in range(32):
        c += 1; st_gb2[k] = c
    st_h = {}; st_B = {}; st_wfree1 = [0] * NGI; st_wfree2 = [0] * NGI
    for gi in range(NGI):
        for tb in range(NTB):
            for fc in range(GF):
                c += 1; st_h[(gi, tb, fc)] = c
            if tb >= 1:
                c += 1; st_B[(gi, tb - 1)] = c
        st_wfree1[gi] = st_h[(gi, NTB - 1, GF - 1)]
        c += 1; st_B[(gi, NTB - 1)] = c
        st_wfree2[gi] = c
    st_total = c

    c = 0
    c += 1; s_rstd = c
    ss_exp = [0] * NT
    for i in range(NT):
        c += 1; ss_exp[i] = c
    ss_hev = {}
    for gi in range(NGI):
        for tb in range(NTB):
            for fc in range(GF):
                c += 1; ss_hev[(gi, tb, fc)] = c
    ss_total = c

    c = 0
    c += 2; v_pre = c
    c += 1; v_negmean = c
    sv_xn_ts = [0] * NT; sv_xn_ev = [0] * NT
    xn_order = [("ts", 0), ("ts", 1)]
    for i in range(2, NT + 2):
        xn_order.append(("ev", i - 2))
        if i < NT:
            xn_order.append(("ts", i))
    for kind, idx in xn_order:
        c += 1
        if kind == "ts":
            sv_xn_ts[idx] = c
        else:
            sv_xn_ev[idx] = c
    c += 1; sv_brb = c
    sv_xs = [0] * NT
    for i in range(NT):
        c += 1; sv_xs[i] = c
    c += 1; sv_qk = c
    sv_b1ev = [0] * E
    for e in range(E):
        c += 1; sv_b1ev[e] = c
    sv_xqe = [0] * 4
    for n in range(4):
        c += 1; sv_xqe[n] = c
    sv_xke = [0] * 4
    for n in range(4):
        c += 1; sv_xke[n] = c
    sv_ve = [0] * 32
    for k in range(32):
        c += 1; sv_ve[k] = c
    sv_aux_done = c
    ENT_T = 12 + N_ITER * 14 + 12
    ENT_PRE = 40
    ENT_PER = 26
    ent_cum = ENT_PRE
    sv_oev0 = [0] * NT; sv_oev = [0] * NT
    sv_gates = None
    if ent_cum >= ENT_T:
        c += 1; sv_gates = c
    for i in range(NT):
        c += 1; sv_oev0[i] = c
        c += 1; sv_oev[i] = c
        prev = ent_cum
        ent_cum = min(ent_cum + ENT_PER, ENT_T)
        if sv_gates is None and ent_cum >= ENT_T:
            c += 1; sv_gates = c
    sv_gTev = [0] * NT
    for i in range(NT):
        c += 1; sv_gTev[i] = c
    sv_gb2 = [0] * 32
    for k in range(32):
        c += 1; sv_gb2[k] = c
    sv_all_A = c
    sv_yev = {}
    for gi in range(NGI):
        for tb in range(NTB):
            c += 1; sv_yev[(gi, tb)] = c
    sv_final = c

    es = ExitStack()
    sb = lambda n, s, d: es.enter_context(nc.sbuf_tensor(n, s, d))
    acc = sb("acc", [128, NT * D], F32)
    xnT = sb("xnT", [128, DC * T], BF16)
    gates = sb("gates", [128, NT * E], F32)
    b1sb = sb("b1sb", [128, E * 32], F32)
    b1raw = sb("b1raw", [32, E * 128], F32)
    ident_bf = sb("ident_bf", [128, 128], BF16)
    ident_f32 = sb("ident_f32", [128, 128], F32)
    wrbf = sb("wrbf", [128, DC * E], BF16)
    brpad = sb("brpad", [128, E], BF16)
    brb_half = sb("brb_half", [128, E], F32)
    e0row = sb("e0row", [128, 128], BF16)
    b2p = sb("b2p", [128, D], BF16)
    gTp = sb("gTp", [128, T], BF16)
    sums = sb("sums", [128, NT], F32)
    negmean = sb("negmean", [128, NT], F32)
    sumsq = sb("sumsq", [128, NT], F32)
    lnv = sb("lnv", [128, NT], F32)
    rstd = sb("rstd", [128, NT], F32)
    junk = sb("junk", [128, D], BF16)
    Xs = sb("Xs", [128, NT * E], F32)
    mx = sb("mx", [128, NT], F32)
    tauL = sb("tauL", [128, NT], F32)
    tauM = sb("tauM", [128, NT], F32)
    sL = sb("sL", [128, NT], F32)
    fLo = sb("fLo", [128, NT], F32)
    sM = sb("sM", [128, NT], F32)
    prod = sb("prod", [128, NT], F32)
    delta = sb("delta", [128, NT], F32)
    rbuf = sb("rbuf", [128, NT * E], F32)
    sqbuf = sb("sqbuf", [128, NT * E], F32)
    pbuf = sb("pbuf", [128, NT * E], F32)
    t1buf = sb("t1buf", [128, NT * E], F32)
    maskbuf = sb("maskbuf", [128, NT * E], F32)
    Zg = sb("Zg", [128, NT], F32)
    Zinv = sb("Zinv", [128, NT], F32)
    mxc4 = sb("mxc4", [128, NT * 4], F32)
    mxr = sb("mxr", [128, NT], F32)
    negmax = sb("negmax", [128, NT], F32)
    zp = sb("zp", [128, NT * 4], F32)
    Zcol = sb("Zcol", [128, NT], F32)
    Zvinv = sb("Zvinv", [128, NT], F32)
    zerocol = sb("zerocol", [128, 1], F32)
    epscol = sb("epscol", [128, 1], F32)
    ccol = sb("ccol", [128, 1], F32)

    sems = ExitStack()
    SY = Ctr(sems, nc, "SY")
    GP = Ctr(sems, nc, "GP")
    WD1 = [Ctr(sems, nc, "WDA0"), Ctr(sems, nc, "WDA1")]
    WD2 = [Ctr(sems, nc, "WDB0"), Ctr(sems, nc, "WDB1")]
    SOUT = Ctr(sems, nc, "SOUT")
    SV = Ctr(sems, nc, "SV")
    SS = Ctr(sems, nc, "SS")
    ST = Ctr(sems, nc, "ST")
    SD = Ctr(sems, nc, "SD")
    SB = Ctr(sems, nc, "SB")
    SB2 = Ctr(sems, nc, "SB2")

    block = es.enter_context(nc.Block())

    esA = ExitStack()
    sbA = lambda n, s, d: esA.enter_context(nc.sbuf_tensor(n, s, d))
    xnb = [sbA("xnb0", [128, D], BF16), sbA("xnb1", [128, D], BF16)]
    qbf = sbA("qbf", [RANK, D], BF16)
    kbf = sbA("kbf", [RANK, D], BF16)
    QTp = sbA("QTp", [128, DC * 128], BF16)
    KTp = sbA("KTp", [128, DC * 128], BF16)
    xqT = sbA("xqT", [128, T], BF16)
    xkT = sbA("xkT", [128, T], BF16)
    v_sb = sbA("v_sb", [128, NT * D], BF16)
    P_sbT = [sbA("P_sbT0", [128, T], BF16), sbA("P_sbT1", [128, T], BF16)]
    vobf = sbA("vobf", [128, DC * D], BF16)
    ones_col = sbA("ones_col", [128, 1], BF16)

    esP1 = ExitStack()
    psA = lambda n, s, d: esP1.enter_context(nc.psum_tensor(n, s, d))
    pt_xn = psA("pt_xn", [128, 1024], BF16)
    pmisc = psA("pmisc", [128, 1024], BF16)
    pmisc32 = psA("pmisc32", [128, 512], F32)
    ps_big = psA("ps_big", [128, 2048], F32)

    @block.sync
    def _(sync):
        SY.inc(sync.dma_start(out=acc.ap().rearrange("p (i d) -> p i d", i=NT),
                              in_=X.rearrange("(i p) d -> p i d", p=128)), 16)
        SY.inc(sync.dma_start(out=b1raw.ap().rearrange("j (e c) -> j e c", e=E),
                              in_=B1.rearrange("e (j c) -> j e c", c=128)), 16)
        SY.inc(sync.dma_start(out=ident_f32[:, :], in_=ID32[:, :]), 16)
    sy_setup = SY.n

    @block.vector
    def _(vector):
        nc.vector.memset(zerocol[:, :], 0.0)
        nc.vector.memset(epscol[:, :], EPS)
        nc.vector.memset(ccol[:, :], -32.0)
        nc.vector.memset(ones_col[:, :], 1.0)
        SV.inc(nc.vector.memset(e0row[:, :], 0.0))
        vector.wait_ge(SV.h, SV.n)
        nc.vector.memset(e0row[0:1, :], 1.0)
        nc.vector.memset(brpad[:, :], 0.0)
        nc.vector.memset(b2p[:, :], 0.0)
        nc.vector.memset(gTp[:, :], 0.0)
        nc.vector.memset(QTp[:, :], 0.0)
        assert SV.inc(nc.vector.memset(KTp[:, :], 0.0)) == v_pre

    @block.gpsimd
    def _(gpsimd):
        GP.inc(gpsimd.dma_start(out=ident_bf[:, :], in_=ID32[:, :]), 16)
        GP.inc(gpsimd.dma_start(out=qbf[:, :], in_=Q[:, :]), 16)
        GP.inc(gpsimd.dma_start(out=kbf[:, :], in_=KI[:, :]), 16)
        GP.inc(gpsimd.dma_start(out=vobf.ap().rearrange("p (dc e) -> p dc e", dc=DC),
                                in_=VO.rearrange("(dc p) e -> p dc e", p=128)), 16)
        GP.inc(gpsimd.dma_start(out=wrbf.ap().rearrange("p (dc e) -> p dc e", dc=DC),
                                in_=WR.rearrange("(dc p) e -> p dc e", p=128)), 16)
        gpsimd.wait_ge(SV.h, v_pre)
        GP.inc(gpsimd.dma_start(out=brpad[0:1, :], in_=BR[:, :]), 16)
        GP.inc(gpsimd.dma_start(out=b2p[0:8, :], in_=B2[:, :]), 16)
    gp_setup = GP.n

    @block.vector
    def _(vector):
        vector.wait_ge(SY.h, sy_setup)
        for i in range(NT):
            ins = nc.vector.reduce_sum(sums[:, i:i + 1], acc[:, i * D:(i + 1) * D], axis=AX.X)
        SB.inc(ins); vector.wait_ge(SB.h, SB.n)
        for i in range(NT):
            ins = nc.vector.tensor_scalar(negmean[:, i:i + 1], sums[:, i:i + 1],
                                          -1.0 / D, None, op0=ALU.mult)
        assert SV.inc(ins) == v_negmean

    @block.scalar
    def _(scalar):
        scalar.wait_ge(SV.h, v_negmean)
        for i in range(NT):
            ins = nc.scalar.activation(junk[:, :], acc[:, i * D:(i + 1) * D], AF.Square,
                                       bias=negmean[:, i:i + 1], scale=1.0,
                                       accum_out=sumsq[:, i:i + 1])
            SB2.inc(ins); scalar.wait_ge(SB2.h, SB2.n)
        for i in range(NT):
            ins = nc.scalar.activation(lnv[:, i:i + 1], sumsq[:, i:i + 1], AF.Ln,
                                       bias=epscol[:, :], scale=1.0 / D)
        SB2.inc(ins); scalar.wait_ge(SB2.h, SB2.n)
        for i in range(NT):
            ins = nc.scalar.activation(rstd[:, i:i + 1], lnv[:, i:i + 1], AF.Exp,
                                       bias=zerocol[:, :], scale=-0.5)
        assert SS.inc(ins) == s_rstd

    @block.vector
    def _(vector):
        vector.wait_ge(SS.h, s_rstd)
        for kind, i in xn_order:
            if kind == "ts":
                if i >= 2:
                    vector.wait_ge(ST.h, st_xnt[i - 2])
                ins = nc.vector.tensor_scalar(xnb[i % 2][:, :], acc[:, i * D:(i + 1) * D],
                                              negmean[:, i:i + 1], rstd[:, i:i + 1],
                                              op0=ALU.add, op1=ALU.mult)
                assert SV.inc(ins) == sv_xn_ts[i]
            else:
                vector.wait_ge(ST.h, st_xnt[i])
                for j in range(DC):
                    ins = nc.vector.tensor_copy(
                        xnT[:, j * T + i * 128: j * T + i * 128 + 128],
                        pt_xn[:, j * 128:(j + 1) * 128])
                assert SV.inc(ins) == sv_xn_ev[i]

    @block.tensor
    def _(tensor):
        tensor.wait_ge(GP.h, gp_setup)
        for i in range(NT):
            tensor.wait_ge(SV.h, sv_xn_ts[i])
            if i >= 1:
                tensor.wait_ge(SV.h, sv_xn_ev[i - 1])
            for j in range(DC):
                ins = nc.tensor.transpose(pt_xn[:, j * 128:(j + 1) * 128],
                                          xnb[i % 2][:, j * 128:(j + 1) * 128],
                                          ident_bf[:, :])
            assert ST.inc(ins) == st_xnt[i]
        ins = nc.tensor.matmul(pmisc32[:, 448:448 + E], e0row[:, :], brpad[:, :],
                               start=True, stop=True)
        assert ST.inc(ins) == st_brbc
        for i in range(NT):
            tensor.wait_ge(SV.h, sv_xn_ev[i])
            if i >= 4:
                tensor.wait_ge(SV.h, sv_xs[i - 4])
            base = (i % 4) * 512
            for dc in range(DC):
                ins = nc.tensor.matmul(ps_big[:, base:base + E],
                                       xnT[:, dc * T + i * 128: dc * T + i * 128 + 128],
                                       wrbf[:, dc * E:(dc + 1) * E],
                                       start=(dc == 0), stop=(dc == DC - 1))
            assert ST.inc(ins) == st_logit[i]
        for dc in range(DC):
            ins = nc.tensor.transpose(pmisc[:, dc * 64:(dc + 1) * 64],
                                      qbf[:, dc * 128:(dc + 1) * 128],
                                      ident_bf[0:64, 0:64])
        assert ST.inc(ins) == st_qt
        for dc in range(DC):
            ins = nc.tensor.transpose(pmisc[:, 512 + dc * 64: 512 + (dc + 1) * 64],
                                      kbf[:, dc * 128:(dc + 1) * 128],
                                      ident_bf[0:64, 0:64])
        assert ST.inc(ins) == st_kt
        tensor.wait_ge(SY.h, sy_setup)
        tensor.wait_ge(SV.h, sv_brb)
        for e in range(E):
            if e >= 1:
                tensor.wait_ge(SV.h, sv_b1ev[e - 1])
            ins = nc.tensor.transpose(pmisc32[:, (e % 4) * 128:(e % 4) * 128 + 32],
                                      b1raw[:, e * 128:(e + 1) * 128],
                                      ident_f32[0:32, 0:32])
            assert ST.inc(ins) == st_b1[e]

    @block.vector
    def _(vector):
        vector.wait_ge(ST.h, st_brbc)
        ins = nc.vector.tensor_scalar(brb_half[:, :], pmisc32[:, 448:448 + E],
                                      0.5, None, op0=ALU.mult)
        assert SV.inc(ins) == sv_brb
        vector.wait_ge(SV.h, sv_brb)
        for i in range(NT):
            vector.wait_ge(ST.h, st_logit[i])
            base = (i % 4) * 512
            ins = nc.vector.scalar_tensor_tensor(Xs[:, i * E:(i + 1) * E],
                                                 ps_big[:, base:base + E],
                                                 0.5, brb_half[:, :],
                                                 op0=ALU.mult, op1=ALU.add)
            assert SV.inc(ins) == sv_xs[i]
        vector.wait_ge(ST.h, st_kt)
        vector.wait_ge(SV.h, v_pre)
        for dc in range(DC):
            nc.vector.tensor_copy(QTp[:, dc * 128: dc * 128 + 64],
                                  pmisc[:, dc * 64:(dc + 1) * 64])
        for dc in range(DC):
            ins = nc.vector.tensor_copy(KTp[:, dc * 128: dc * 128 + 64],
                                        pmisc[:, 512 + dc * 64: 512 + (dc + 1) * 64])
        assert SV.inc(ins) == sv_qk
        for e in range(E):
            vector.wait_ge(ST.h, st_b1[e])
            ins = nc.vector.tensor_copy(b1sb[:, e * 32:(e + 1) * 32],
                                        pmisc32[:, (e % 4) * 128:(e % 4) * 128 + 32])
            assert SV.inc(ins) == sv_b1ev[e]

    if PH < 4:
        return _finish_early(nc, block, OUT, acc, SV, SOUT, NT)
    @block.tensor
    def _(tensor):
        tensor.wait_ge(SV.h, sv_xs[NT - 1])
        tensor.wait_ge(SV.h, sv_qk)
        for n in range(4):
            for dc in range(DC):
                ins = nc.tensor.matmul(ps_big[:, n * 512:(n + 1) * 512],
                                       QTp[:, dc * 128:(dc + 1) * 128],
                                       xnT[:, dc * T + n * 512: dc * T + (n + 1) * 512],
                                       start=(dc == 0), stop=(dc == DC - 1))
            assert ST.inc(ins) == st_xq[n]
        for n in range(4):
            tensor.wait_ge(SV.h, sv_xqe[n])
            for dc in range(DC):
                ins = nc.tensor.matmul(ps_big[:, n * 512:(n + 1) * 512],
                                       KTp[:, dc * 128:(dc + 1) * 128],
                                       xnT[:, dc * T + n * 512: dc * T + (n + 1) * 512],
                                       start=(dc == 0), stop=(dc == DC - 1))
            assert ST.inc(ins) == st_xk[n]
        for k in range(32):
            ut, eh = divmod(k, 2)
            slot = k % 4
            tensor.wait_ge(SV.h, sv_xke[slot] if k < 4 else sv_ve[k - 4])
            for dc in range(DC):
                ins = nc.tensor.matmul(
                    ps_big[:, slot * 512:(slot + 1) * 512],
                    xnT[:, dc * T + ut * 128: dc * T + ut * 128 + 128],
                    vobf[:, dc * D + eh * 512: dc * D + eh * 512 + 512],
                    start=(dc == 0), stop=(dc == DC - 1))
            assert ST.inc(ins) == st_v[k]

    @block.vector
    def _(vector):
        for n in range(4):
            vector.wait_ge(ST.h, st_xq[n])
            ins = nc.vector.tensor_scalar(xqT[:, n * 512:(n + 1) * 512],
                                          ps_big[:, n * 512:(n + 1) * 512],
                                          SCALE, None, op0=ALU.mult)
            assert SV.inc(ins) == sv_xqe[n]
        for n in range(4):
            vector.wait_ge(ST.h, st_xk[n])
            ins = nc.vector.tensor_copy(xkT[:, n * 512:(n + 1) * 512],
                                        ps_big[:, n * 512:(n + 1) * 512])
            assert SV.inc(ins) == sv_xke[n]
        for k in range(32):
            ut, eh = divmod(k, 2)
            slot = k % 4
            vector.wait_ge(ST.h, st_v[k])
            ins = nc.vector.tensor_copy(
                v_sb[:, ut * D + eh * 512: ut * D + eh * 512 + 512],
                ps_big[:, slot * 512:(slot + 1) * 512])
            assert SV.inc(ins) == sv_ve[k]
        assert SV.n == sv_aux_done

    if PH < 5:
        return _finish_early(nc, block, OUT, acc, SV, SOUT, NT)
    def vbar(vector, ins):
        SV.inc(ins)
        vector.wait_ge(SV.h, SV.n)

    ent_ops = []

    def _build_ent_ops():
        HT = NT // 2
        HW = HT * E
        X3 = [Xs[:, h * HW:(h + 1) * HW].rearrange("p (i e) -> p i e", e=E)
              for h in (0, 1)]
        D3 = [rbuf[:, h * HW:(h + 1) * HW].rearrange("p (i e) -> p i e", e=E)
              for h in (0, 1)]
        SQ3 = [sqbuf[:, h * HW:(h + 1) * HW].rearrange("p (i e) -> p i e", e=E)
               for h in (0, 1)]
        P3 = [pbuf[:, h * HW:(h + 1) * HW].rearrange("p (i e) -> p i e", e=E)
              for h in (0, 1)]
        T3 = [t1buf[:, h * HW:(h + 1) * HW].rearrange("p (i e) -> p i e", e=E)
              for h in (0, 1)]
        G3 = [gates[:, h * HW:(h + 1) * HW].rearrange("p (i e) -> p i e", e=E)
              for h in (0, 1)]
        col = lambda buf, h: buf[:, h * HT:(h + 1) * HT]
        c3 = lambda buf, h: buf[:, h * HT:(h + 1) * HT].unsqueeze(-1)
        bc = lambda buf, h: buf[:, h * HT:(h + 1) * HT].unsqueeze(-1).broadcast_to([128, HT, E])
        A = ent_ops.append
        for h in (0, 1):
            A(lambda h=h: nc.vector.reduce_max(c3(mx, h), X3[h], axis=AX.X))
        for h in (0, 1):
            A(lambda h=h: nc.vector.tensor_scalar(col(tauL, h), col(mx, h), -1.0, None, op0=ALU.add))
        for h in (0, 1):
            A(lambda h=h: nc.vector.tensor_tensor(D3[h], X3[h], bc(tauL, h), op=ALU.subtract))
        for h in (0, 1):
            A(lambda h=h: nc.vector.scalar_tensor_tensor(SQ3[h], D3[h], 0.0, D3[h],
                                                         op0=ALU.max, op1=ALU.mult))
        for h in (0, 1):
            A(lambda h=h: nc.vector.reduce_sum(c3(sL, h), SQ3[h], axis=AX.X))
        for h in (0, 1):
            A(lambda h=h: nc.vector.tensor_scalar(col(fLo, h), col(sL, h), -1.0, None, op0=ALU.add))
        dm = 1.0 - (1.0 / E) ** 0.5
        for it in range(N_ITER):
            dm = dm * 0.5
            for h in (0, 1):
                A(lambda h=h, dm=dm: nc.vector.tensor_scalar(col(tauM, h), col(tauL, h), dm, None, op0=ALU.add))
            for h in (0, 1):
                A(lambda h=h: nc.vector.tensor_tensor(D3[h], X3[h], bc(tauM, h), op=ALU.subtract))
            for h in (0, 1):
                A(lambda h=h: nc.vector.scalar_tensor_tensor(SQ3[h], D3[h], 0.0, D3[h],
                                                             op0=ALU.max, op1=ALU.mult))
            for h in (0, 1):
                A(lambda h=h: nc.vector.reduce_sum(c3(sM, h), SQ3[h], axis=AX.X))
            for h in (0, 1):
                A(lambda h=h: nc.vector.scalar_tensor_tensor(col(prod, h), col(sM, h), -1.0,
                                                             col(fLo, h), op0=ALU.add, op1=ALU.mult))
            for h in (0, 1):
                A(lambda h=h, dm=dm: nc.vector.tensor_scalar(col(delta, h), col(prod, h), 0.0, dm,
                                                             op0=ALU.is_ge, op1=ALU.mult))
            for h in (0, 1):
                A(lambda h=h: nc.vector.tensor_tensor(col(tauL, h), col(tauL, h), col(delta, h),
                                                      op=ALU.add))
        for h in (0, 1):
            A(lambda h=h: nc.vector.tensor_tensor(D3[h], X3[h], bc(tauM, h), op=ALU.subtract))
        for h in (0, 1):
            A(lambda h=h: nc.vector.scalar_tensor_tensor(P3[h], D3[h], 0.0, D3[h],
                                                         op0=ALU.max, op1=ALU.mult))
        for h in (0, 1):
            A(lambda h=h: nc.vector.reduce_sum(c3(Zg, h), P3[h], axis=AX.X))
        for h in (0, 1):
            A(lambda h=h: nc.vector.reciprocal(col(Zinv, h), col(Zg, h)))
        for h in (0, 1):
            A(lambda h=h: nc.vector.tensor_tensor(T3[h], P3[h], bc(Zinv, h), op=ALU.mult))
        for h in (0, 1):
            A(lambda h=h: nc.vector.scalar_tensor_tensor(G3[h], T3[h], THRESH, T3[h],
                                                         op0=ALU.is_gt, op1=ALU.mult))

    _build_ent_ops()
    assert len(ent_ops) == ENT_T
    ent_state = {"drained": 0}

    def drain_ent(k):
        d = ent_state["drained"]
        for j in range(d, min(d + k, ENT_T)):
            ins = ent_ops[j]()
            if j == ENT_T - 1:
                assert SV.inc(ins) == sv_gates
        ent_state["drained"] = min(d + k, ENT_T)

    if PH < 6:
        return _finish_early(nc, block, OUT, acc, SV, SOUT, NT)
    esP1.close()
    esP2 = ExitStack()
    psB = lambda n, s, d: esP2.enter_context(nc.psum_tensor(n, s, d))
    ps_s = psB("ps_s", [128, 2048], F32)
    ps_oz = psB("ps_oz", [128, 1536], F32)

    @block.tensor
    def _(tensor):
        tensor.wait_ge(SV.h, sv_aux_done)

        def emit_scores(i):
            if i >= 1:
                tensor.wait_ge(SS.h, ss_exp[i - 1])
            for uc in range(NT):
                ins = nc.tensor.matmul(ps_s[:, uc * 128:(uc + 1) * 128],
                                       xkT[:, uc * 128:(uc + 1) * 128],
                                       xqT[:, i * 128:(i + 1) * 128],
                                       start=True, stop=True)
            assert ST.inc(ins) == st_sc[i]

        emit_scores(0)
        for i in range(NT):
            if i + 1 < NT:
                emit_scores(i + 1)
            tensor.wait_ge(SS.h, ss_exp[i])
            if i >= 1:
                tensor.wait_ge(SV.h, sv_oev0[i - 1])
            first_eh1 = True
            for uc in range(NT):
                pt = P_sbT[i % 2][:, uc * 128:(uc + 1) * 128]
                nc.tensor.matmul(ps_oz[:, 0:512], pt,
                                 v_sb[:, uc * D: uc * D + 512],
                                 start=(uc == 0), stop=(uc == NT - 1))
                nc.tensor.matmul(ps_oz[:, 1024:1025], pt, ones_col[:, 0:1],
                                 start=(uc == 0), stop=(uc == NT - 1))
                if first_eh1 and i >= 1:
                    tensor.wait_ge(SV.h, sv_oev[i - 1])
                first_eh1 = False
                ins = nc.tensor.matmul(ps_oz[:, 512:1024], pt,
                                       v_sb[:, uc * D + 512: uc * D + 1024],
                                       start=(uc == 0), stop=(uc == NT - 1))
            assert ST.inc(ins) == st_pv[i]

    @block.scalar
    def _(scalar):
        scalar.wait_ge(SV.h, v_pre)
        for i in range(NT):
            scalar.wait_ge(ST.h, st_sc[i])
            if i >= 2:
                scalar.wait_ge(ST.h, st_pv[i - 2])
            for n in range(4):
                ins = nc.scalar.activation(P_sbT[i % 2][:, n * 512:(n + 1) * 512],
                                           ps_s[:, n * 512:(n + 1) * 512], AF.Exp,
                                           bias=ccol[:, :], scale=1.0)
            assert SS.inc(ins) == ss_exp[i]

    @block.vector
    def _(vector):
        drain_ent(ENT_PRE)
        for i in range(NT):
            vector.wait_ge(ST.h, st_pv[i])
            ins = nc.vector.reciprocal(Zvinv[:, i:i + 1], ps_oz[:, 1024:1025])
            SB.inc(ins); vector.wait_ge(SB.h, SB.n)
            ins = nc.vector.scalar_tensor_tensor(
                acc[:, i * D: i * D + 512],
                ps_oz[:, 0:512],
                Zvinv[:, i:i + 1],
                acc[:, i * D: i * D + 512],
                op0=ALU.mult, op1=ALU.add)
            assert SV.inc(ins) == sv_oev0[i]
            ins = nc.vector.scalar_tensor_tensor(
                acc[:, i * D + 512: i * D + 1024],
                ps_oz[:, 512:1024],
                Zvinv[:, i:i + 1],
                acc[:, i * D + 512: i * D + 1024],
                op0=ALU.mult, op1=ALU.add)
            assert SV.inc(ins) == sv_oev[i]
            drain_ent(ENT_PER)

    if PH < 7:
        return _finish_early(nc, block, OUT, acc, SV, SOUT, NT)
    @block.tensor
    def _(tensor):
        tensor.wait_ge(SV.h, sv_gates)
        tensor.wait_ge(SS.h, ss_exp[NT - 1])
        for i in range(NT):
            if i >= 4:
                tensor.wait_ge(SV.h, sv_gTev[i - 4])
            ins = nc.tensor.transpose(ps_s[0:8, (i % 4) * 512:(i % 4) * 512 + 128],
                                      gates[:, i * E:(i + 1) * E], ident_f32[:, :])
            assert ST.inc(ins) == st_gT[i]
        for k in range(32):
            ti, dh = divmod(k, 2)
            tensor.wait_ge(SV.h, sv_gTev[ti])
            if k < 2:
                tensor.wait_ge(SV.h, sv_oev[NT - 1])
            else:
                tensor.wait_ge(SV.h, sv_gb2[k - 2])
            ins = nc.tensor.matmul(ps_oz[:, dh * 512:(dh + 1) * 512],
                                   gTp[:, ti * 128:(ti + 1) * 128],
                                   b2p[:, dh * 512:(dh + 1) * 512],
                                   start=True, stop=True)
            assert ST.inc(ins) == st_gb2[k]

    @block.vector
    def _(vector):
        vector.wait_ge(SV.h, v_pre)
        for i in range(NT):
            vector.wait_ge(ST.h, st_gT[i])
            ins = nc.vector.tensor_copy(gTp[0:8, i * 128:(i + 1) * 128],
                                        ps_s[0:8, (i % 4) * 512:(i % 4) * 512 + 128])
            assert SV.inc(ins) == sv_gTev[i]
        for k in range(32):
            ti, dh = divmod(k, 2)
            vector.wait_ge(ST.h, st_gb2[k])
            ins = nc.vector.tensor_tensor(
                acc[:, ti * D + dh * 512: ti * D + dh * 512 + 512],
                acc[:, ti * D + dh * 512: ti * D + dh * 512 + 512],
                ps_oz[:, dh * 512:(dh + 1) * 512], op=ALU.add)
            assert SV.inc(ins) == sv_gb2[k]
        assert SV.n == sv_all_A

    if dbg:
        @block.sync
        def _(sync):
            sync.wait_ge(SV.h, sv_all_A)
            SD.inc(sync.dma_start(out=DBG_G[:, :], in_=gates[:, :]), 16)
            SD.inc(sync.dma_start(out=DBG_XS[:, :], in_=Xs[:, :]), 16)
            SD.inc(sync.dma_start(out=DBG_A.rearrange("(i p) d -> p i d", p=128),
                                  in_=acc.ap().rearrange("p (i d) -> p i d", i=NT)), 16)
    sd_dbg = SD.n

    if PH < 8:
        return _finish_early(nc, block, OUT, acc, SV, SOUT, NT)
    esP2.close()
    esA.close()
    esB = ExitStack()
    sbB = lambda n, s, d: esB.enter_context(nc.sbuf_tensor(n, s, d))
    psC = lambda n, s, d: esB.enter_context(nc.psum_tensor(n, s, d))
    w1bf = [sbB("w1bf0", [128, GF * D], BF16), sbB("w1bf1", [128, GF * D], BF16)]
    w2bf = [sbB("w2bf0", [128, GF * D], BF16), sbB("w2bf1", [128, GF * D], BF16)]
    h_grp = [sbB("hg0", [128, GF * TB], BF16), sbB("hg1", [128, GF * TB], BF16)]
    ps_h = [psC("ps_h0", [128, 512], F32), psC("ps_h1", [128, 512], F32)]
    ps_y = psC("ps_y", [128, 2048], F32)

    @block.gpsimd
    def _(gpsimd):
        for gi in range(NGI):
            par = gi % 2
            if gi < 2:
                gpsimd.wait_ge(SV.h, sv_all_A)
                if dbg:
                    gpsimd.wait_ge(SD.h, sd_dbg)
            else:
                gpsimd.wait_ge(ST.h, st_wfree1[gi - 2])
            WD1[par].inc(gpsimd.dma_start(out=w1bf[par].ap(), in_=W1[gi, :, :]), 16)
            if gi >= 2:
                gpsimd.wait_ge(ST.h, st_wfree2[gi - 2])
            WD2[par].inc(gpsimd.dma_start(out=w2bf[par].ap(), in_=W2[gi, :, :]), 16)

    @block.tensor
    def _(tensor):
        tensor.wait_ge(SV.h, sv_all_A)

        def emit_A(gi, tb):
            par = gi % 2
            for fc in range(GF):
                lin = (gi * NTB + tb) * GF + fc - 2
                if lin >= 0:
                    pgi, rem = divmod(lin, NTB * GF)
                    ptb, pfc = divmod(rem, GF)
                    tensor.wait_ge(SS.h, ss_hev[(pgi, ptb, pfc)])
                for dc in range(DC):
                    ins = nc.tensor.matmul(
                        ps_h[fc % 2][:, 0:TB],
                        w1bf[par][:, dc * 1024 + fc * 128: dc * 1024 + fc * 128 + 128],
                        xnT[:, dc * T + tb * TB: dc * T + (tb + 1) * TB],
                        start=(dc == 0), stop=(dc == DC - 1))
                assert ST.inc(ins) == st_h[(gi, tb, fc)]

        def emit_B(gi, tb):
            par = gi % 2
            tensor.wait_ge(SS.h, ss_hev[(gi, tb, GF - 1)])
            if not (gi == 0 and tb == 0):
                pgi, ptb = (gi, tb - 1) if tb >= 1 else (gi - 1, NTB - 1)
                tensor.wait_ge(SV.h, sv_yev[(pgi, ptb)])
            for fc in range(GF):
                for tq in range(2):
                    for dh in range(2):
                        ins = nc.tensor.matmul(
                            ps_y[:, (tq * 2 + dh) * 512:(tq * 2 + dh + 1) * 512],
                            h_grp[tb % 2][:, fc * TB + tq * 128: fc * TB + tq * 128 + 128],
                            w2bf[par][:, fc * 1024 + dh * 512: fc * 1024 + dh * 512 + 512],
                            start=(fc == 0), stop=(fc == GF - 1))
            assert ST.inc(ins) == st_B[(gi, tb)]

        for gi in range(NGI):
            par = gi % 2
            tensor.wait_ge(WD1[par].h, 16 * (gi // 2 + 1))
            emit_A(gi, 0)
            for tb in range(1, NTB):
                emit_A(gi, tb)
                if tb == 1:
                    tensor.wait_ge(WD2[par].h, 16 * (gi // 2 + 1))
                emit_B(gi, tb - 1)
            emit_B(gi, NTB - 1)
        assert ST.n == st_total

    @block.scalar
    def _(scalar):
        for gi in range(NGI):
            e, g = divmod(gi, NG)
            for tb in range(NTB):
                for fc in range(GF):
                    scalar.wait_ge(ST.h, st_h[(gi, tb, fc)])
                    lin = gi * NTB + tb - 2
                    if lin >= 0:
                        pgi, ptb = divmod(lin, NTB)
                        scalar.wait_ge(ST.h, st_B[(pgi, ptb)])
                    ins = nc.scalar.activation(
                        h_grp[tb % 2][:, fc * TB:(fc + 1) * TB],
                        ps_h[fc % 2][:, 0:TB], AF.Relu,
                        bias=b1sb[:, e * 32 + g * GF + fc: e * 32 + g * GF + fc + 1],
                        scale=1.0)
                    assert SS.inc(ins) == ss_hev[(gi, tb, fc)]
        assert SS.n == ss_total

    @block.vector
    def _(vector):
        for gi in range(NGI):
            e, g = divmod(gi, NG)
            for tb in range(NTB):
                vector.wait_ge(ST.h, st_B[(gi, tb)])
                for tq in range(2):
                    ti = tb * 2 + tq
                    for dh in range(2):
                        ins = nc.vector.scalar_tensor_tensor(
                            acc[:, ti * D + dh * 512: ti * D + dh * 512 + 512],
                            ps_y[:, (tq * 2 + dh) * 512:(tq * 2 + dh + 1) * 512],
                            gates[:, ti * E + e: ti * E + e + 1],
                            acc[:, ti * D + dh * 512: ti * D + dh * 512 + 512],
                            op0=ALU.mult, op1=ALU.add)
                assert SV.inc(ins) == sv_yev[(gi, tb)]
        assert SV.n == sv_final

    @block.sync
    def _(sync):
        for tb in range(NTB):
            sync.wait_ge(SV.h, sv_yev[(NGI - 1, tb)])
            SOUT.inc(sync.dma_start(
                out=OUT[tb * TB:(tb + 1) * TB, :].rearrange("(i p) d -> p i d", p=128),
                in_=acc[:, 2 * tb * D:(2 * tb + 2) * D].rearrange("p (i d) -> p i d", i=2)), 16)
        sync.wait_ge(SOUT.h, 16 * NTB)

    esB.close()
    sems.close()
    es.close()
    return nc


def kernel(L0_attn_x, L0_attn_Q, L0_attn_K, L0_attn_VO, L0_ffn_Wr, L0_ffn_br,
           L0_ffn_W1, L0_ffn_b1, L0_ffn_W2, L0_ffn_b2, _dbg=False, _cores=8):
    x = np.ascontiguousarray(np.asarray(L0_attn_x, dtype=np.float32))
    common = {
        "q": np.ascontiguousarray(np.asarray(L0_attn_Q, np.float32)),
        "k": np.ascontiguousarray(np.asarray(L0_attn_K, np.float32)),
        "vo": np.ascontiguousarray(np.asarray(L0_attn_VO, np.float32)),
        "wr": np.ascontiguousarray(np.asarray(L0_ffn_Wr, np.float32)),
        "br": np.ascontiguousarray(np.asarray(L0_ffn_br, np.float32)).reshape(1, E),
        "w1": np.ascontiguousarray(
            np.asarray(L0_ffn_W1, np.float32).reshape(E, DC, 128, NG, 1024)
            .transpose(0, 3, 2, 1, 4)).reshape(NGI, 128, DC * 1024),
        "b1": np.ascontiguousarray(np.asarray(L0_ffn_b1, np.float32)),
        "w2": np.ascontiguousarray(
            np.asarray(L0_ffn_W2, np.float32).reshape(E, NG, GF, 128, 1024)
            .transpose(0, 1, 3, 2, 4)).reshape(NGI, 128, GF * 1024),
        "b2": np.ascontiguousarray(np.asarray(L0_ffn_b2, np.float32)),
        "id32": np.eye(128, dtype=np.float32),
    }
    nc = build(bass.Bass("TRN2", target_bir_lowering=False), dbg=_dbg)
    in_maps = [dict(common, x=np.ascontiguousarray(x[c])) for c in range(_cores)]
    res = run_bass_kernel_spmd(nc, in_maps, core_ids=list(range(_cores)))
    out = np.stack([np.asarray(res.results[c]["out"]) for c in range(_cores)], axis=0)
    if _dbg:
        return out, res
    return out
